# version 12
# baseline (speedup 1.0000x reference)
import math
from contextlib import ExitStack

import numpy as np
import ml_dtypes

import concourse.bass as bass
import concourse.mybir as mybir
import concourse.bass_utils as _bass_utils
from concourse.bass_utils import run_bass_kernel_spmd

if not getattr(_bass_utils, "_ldw_opt_patched", False):
    _orig_run_command = _bass_utils.run_command

    def _run_command_ldw(argv, **kwargs):
        argv = ["--enable-ldw-opt=true" if a == "--enable-ldw-opt=false" else a
                for a in argv]
        return _orig_run_command(argv, **kwargs)

    _bass_utils.run_command = _run_command_ldw
    _bass_utils._ldw_opt_patched = True


F32 = mybir.dt.float32
BF16 = mybir.dt.bfloat16
AF = mybir.ActivationFunctionType
ALU = mybir.AluOpType
AX = mybir.AxisListType

D = 2048
T = 4096
B = 4
NH = 16
II = 8192
NTOK = 2048
NDC = 16
CAP = [2, 4, 6, 8]
GSETS = [[0, 3, 4, 7], [1, 2, 5, 6]]
SCALE = 1.0 / math.sqrt(128.0)
EPS = 1e-5
INV_D = 1.0 / 2048.0

PHASES = ["A1", "BQ", "A2", "BK", "BV", "C", "D", "F0", "F1", "F2"]


class Plan:

    def __init__(self, nc):
        self.nc = nc
        self.events = []
        self.sems = {}
        self.last_inc_by_engine = {}
        self.phase = "init"
        self.phases = []

    def sem(self, name):
        if name not in self.sems:
            self.sems[name] = self.nc.alloc_semaphore(name)
        return name

    def add(self, engine, fn, deps=(), inc=None, dma=False, small=False):
        eid = len(self.events)
        deps = tuple(d for d in deps if d is not None)
        for d in deps:
            assert 0 <= d < eid, (d, eid)
        if inc is not None:
            self.sem(inc[0])
            self.last_inc_by_engine[engine] = eid
        else:
            assert not dma
        self.events.append((engine, fn, deps, inc, dma or small))
        self.phases.append(self.phase)
        return eid

    def join(self):
        return list(self.last_inc_by_engine.values())

    def emit_all(self, block):
        cum = {}
        after = [None] * len(self.events)
        for i, (eng, fn, deps, inc, dma) in enumerate(self.events):
            if inc is not None:
                s, n = inc
                cum[s] = cum.get(s, 0) + n
                after[i] = (s, cum[s])
        self.final_counts = dict(cum)

        def emitter(E):
            def run(engobj):
                waited = {}
                cur_scope = None
                for i, (eng, fn, deps, inc, dma) in enumerate(self.events):
                    if eng != E:
                        continue
                    ph = self.phases[i]
                    if ph != cur_scope:
                        if cur_scope is not None:
                            self.nc.pop_named_scope(cur_scope)
                        self.nc.push_named_scope(ph)
                        cur_scope = ph
                    for d in deps:
                        deng, _, _, dinc, ddma = self.events[d]
                        if deng == E and not ddma:
                            continue
                        assert after[d] is not None, f"dep {d} of {i} has no inc"
                        s, v = after[d]
                        if waited.get(s, 0) < v:
                            engobj.wait_ge(self.sems[s], v)
                            waited[s] = v
                    inst = fn(engobj) if fn is not None else None
                    if inc is not None:
                        assert inst is not None
                        s, n = inc
                        inst.then_inc(self.sems[s], n)
                if cur_scope is not None:
                    self.nc.pop_named_scope(cur_scope)
            return run

        block.sync(emitter("sync"))
        block.gpsimd(emitter("gpsimd"))
        block.tensor(emitter("tensor"))
        block.vector(emitter("vector"))
        block.scalar(emitter("scalar"))


def build_nc(upto="F2", dump_scr=()):
    nc = bass.Bass("TRN2", target_bir_lowering=False)
    active = set(PHASES[:PHASES.index(upto) + 1])

    x_own = nc.dram_tensor("x_own", [NTOK, D], F32, kind="ExternalInput").ap()
    x_seq = nc.dram_tensor("x_seq", [T, D], F32, kind="ExternalInput").ap()
    wqT = nc.dram_tensor("wqT", [D, D], F32, kind="ExternalInput").ap()
    wkT = nc.dram_tensor("wkT", [D, D], F32, kind="ExternalInput").ap()
    wvT = nc.dram_tensor("wvT", [D, D], F32, kind="ExternalInput").ap()
    woT = nc.dram_tensor("woT", [D, D], F32, kind="ExternalInput").ap()
    wg_t = nc.dram_tensor("wg_t", [4, 16, 16, 128, 128], F32, kind="ExternalInput").ap()
    wu_t = nc.dram_tensor("wu_t", [4, 16, 16, 128, 128], F32, kind="ExternalInput").ap()
    wd_t = nc.dram_tensor("wd_t", [16, 64, 128, 128], F32, kind="ExternalInput").ap()
    masks = nc.dram_tensor("masks", [4, 4, 128, 1024], BF16, kind="ExternalInput").ap()
    out = nc.dram_tensor("out", [NTOK, D], F32, kind="ExternalOutput").ap()

    def scr_tensor(name, shape, dtype):
        kind = "ExternalOutput" if name in dump_scr else "Internal"
        return nc.dram_tensor(name, shape, dtype, kind=kind).ap()

    q_scr = scr_tensor("q_scr", [D, NTOK], BF16)
    k_scr = scr_tensor("k_scr", [D, T], BF16)
    z_scr = scr_tensor("z_scr", [D, NTOK], BF16)
    x1_scr = scr_tensor("x1_scr", [NTOK, D], F32)
    xln2_scr = scr_tensor("xln2_scr", [NTOK, D], BF16)
    hs_scr = scr_tensor("hs_scr", [4, II, 512], BF16)
    wd_bf_scr = scr_tensor("wd_bf_scr", [16, 64, 128, 128], BF16)

    identb_t = nc.inline_tensor(np.eye(128, dtype=ml_dtypes.bfloat16), "identb")
    identf_t = nc.inline_tensor(np.eye(128, dtype=np.float32), "identf")
    ones_t = nc.inline_tensor(np.ones((128, 1), dtype=ml_dtypes.bfloat16), "onescol")
    eps_t = nc.inline_tensor(np.full((128, 4), EPS, dtype=np.float32), "epscol")

    P = Plan(nc)
    for s in ("IN", "W", "SC", "SR", "PE", "ACT", "VE", "OUT", "PL"):
        P.sem(s)

    with ExitStack() as root:
        identb = root.enter_context(nc.sbuf_tensor("identb_sb", [128, 128], BF16))
        identf = root.enter_context(nc.sbuf_tensor("identf_sb", [128, 128], F32))
        onescol = root.enter_context(nc.sbuf_tensor("ones_sb", [128, 1], BF16))
        eps_sb = root.enter_context(nc.sbuf_tensor("eps_sb", [128, 4], F32))
        stats = root.enter_context(nc.sbuf_tensor("stats", [128, 64], F32))
        junk = root.enter_context(nc.sbuf_tensor("junk", [128, 2048], BF16))

        def stc(idx):
            return stats[:, idx:idx + 1]

        P.add("gpsimd", lambda g: g.dma_start(identb[:, :], identb_t.ap()),
              inc=("IN", 16), dma=True)
        P.add("gpsimd", lambda g: g.dma_start(identf[:, :], identf_t.ap()),
              inc=("IN", 16), dma=True)
        P.add("gpsimd", lambda g: g.dma_start(onescol[:, :], ones_t.ap()),
              inc=("IN", 16), dma=True)
        e_const = P.add("gpsimd", lambda g: g.dma_start(eps_sb[:, :], eps_t.ap()),
                        inc=("IN", 16), dma=True)
        gate0 = [e_const]

        def psum_set(names_shapes):
            hs = []
            with ExitStack() as st:
                for nm, shape, dt in names_shapes:
                    hs.append(st.enter_context(nc.psum_tensor(nm, shape, dt)))
            return hs

        def ln_stats_events(src_regions, par, extra_deps=()):
            o = 16 * (par % 4)
            s1, s2 = stc(o + 0), stc(o + 1)
            m, msq = stc(o + 2), stc(o + 3)
            var, lnv, rstd = stc(o + 4), stc(o + 5), stc(o + 6)
            s1c = stats[:, o + 8:o + 12]
            s2c = stats[:, o + 12:o + 16]
            nr = len(src_regions)
            ve = []
            ac = []
            if nr == 1:
                ap, dep = src_regions[0]
                ve.append(P.add("vector",
                                lambda v, ap=ap: v.reduce_sum(s1, ap, AX.X),
                                deps=[dep] + list(extra_deps), inc=("VE", 1), small=True))
                ac.append(P.add("scalar",
                                lambda s, ap=ap: s.activation(
                                    junk[:, 0:ap.shape[-1]], ap, AF.Square, accum_out=s2),
                                deps=[dep] + list(extra_deps), inc=("ACT", 1), small=True))
                e_s2ready = ac[-1]
                s2src = s2
            else:
                for r, (ap, dep) in enumerate(src_regions):
                    ve.append(P.add("vector",
                                    lambda v, ap=ap, r=r: v.reduce_sum(
                                        s1c[:, r:r + 1], ap, AX.X),
                                    deps=[dep] + list(extra_deps), inc=("VE", 1), small=True))
                    ac.append(P.add("scalar",
                                    lambda s, ap=ap, r=r: s.activation(
                                        junk[:, 0:ap.shape[-1]], ap, AF.Square,
                                        accum_out=s2c[:, r:r + 1]),
                                    deps=[dep, ve[-1]] + list(extra_deps),
                                    inc=("ACT", 1), small=True))
                ve.append(P.add("vector", lambda v: v.reduce_sum(s1, s1c, AX.X),
                                deps=[ve[-1]], inc=("VE", 1), small=True))
                e_s2ready = P.add("vector", lambda v: v.reduce_sum(s2, s2c, AX.X),
                                  deps=[ac[-1]], inc=("VE", 1), small=True)
                ve.append(e_s2ready)
                s2src = s2
            e_m = P.add("vector", lambda v: v.tensor_single_scalar(m, s1, INV_D, ALU.mult),
                        deps=[ve[-1]], inc=("VE", 1), small=True)
            e_msq = P.add("vector", lambda v: v.tensor_mul(msq, m, m),
                          deps=[e_m], inc=("VE", 1), small=True)
            e_var = P.add("vector",
                          lambda v: v.tensor_scalar(var, s2src, INV_D, msq,
                                                    ALU.mult, ALU.subtract),
                          deps=[e_msq, e_s2ready], inc=("VE", 1), small=True)
            e_ln = P.add("scalar", lambda s: s.activation(lnv, var, AF.Ln,
                                                          bias=eps_sb[:, 0:1]),
                         deps=[e_var], inc=("ACT", 1), small=True)
            e_rs = P.add("scalar", lambda s: s.activation(rstd, lnv, AF.Exp, scale=-0.5),
                         deps=[e_ln], inc=("ACT", 1), small=True)
            return m, rstd, e_rs

        def phase_ln_transpose(src, ntiles, dst, dst_big, xg, lng, pb, gate, gsz):
            nbuf = len(xg)
            ngroups = ntiles // gsz
            e_norm = {}
            e_ev = {}
            e_grp_done = {}
            for gi in range(ngroups):
                par = gi % nbuf
                b = 32 * par
                s1g, s2g = stats[:, b:b + gsz], stats[:, b + 4:b + 4 + gsz]
                mg, msqg = stats[:, b + 8:b + 8 + gsz], stats[:, b + 12:b + 12 + gsz]
                varg = stats[:, b + 16:b + 16 + gsz]
                lnvg = stats[:, b + 20:b + 20 + gsz]
                rstdg = stats[:, b + 24:b + 24 + gsz]
                deps = list(gate)
                if gi >= nbuf:
                    deps.append(e_grp_done[gi - nbuf])
                e_load = P.add("sync",
                               lambda sy, gi=gi, par=par: sy.dma_start(
                                   xg[par][:, :].rearrange("p (n d) -> p n d", d=D),
                                   src[gi * gsz * 128:(gi + 1) * gsz * 128, :]
                                   .rearrange("(n p) d -> p n d", p=128)),
                               deps=deps, inc=("IN", 16), dma=True)
                e_red = P.add("vector",
                              lambda v, par=par, s1g=s1g: v.reduce_sum(
                                  s1g.rearrange("p (n x) -> p n x", x=1),
                                  xg[par][:, :].rearrange("p (n d) -> p n d", d=D),
                                  AX.X),
                              deps=[e_load], inc=("VE", 1), small=True)
                e_sq = None
                for r in range(gsz):
                    e_sq = P.add("scalar",
                                 lambda sc, par=par, r=r, s2g=s2g: sc.activation(
                                     junk[:, 0:D], xg[par][:, r * D:(r + 1) * D],
                                     AF.Square, accum_out=s2g[:, r:r + 1]),
                                 deps=[e_load], inc=("ACT", 1), small=True)
                e_m = P.add("vector",
                            lambda v, s1g=s1g, mg=mg: v.tensor_single_scalar(
                                mg, s1g, INV_D, ALU.mult),
                            deps=[e_red], inc=("VE", 1), small=True)
                e_msq = P.add("vector",
                              lambda v, mg=mg, msqg=msqg: v.tensor_mul(msqg, mg, mg),
                              deps=[e_m], inc=("VE", 1), small=True)
                e_v0 = P.add("vector",
                             lambda v, s2g=s2g, varg=varg: v.tensor_single_scalar(
                                 varg, s2g, INV_D, ALU.mult),
                             deps=[e_sq], inc=("VE", 1), small=True)
                e_v1 = P.add("vector",
                             lambda v, msqg=msqg, varg=varg: v.tensor_sub(
                                 varg, varg, msqg),
                             deps=[e_msq, e_v0], inc=("VE", 1), small=True)
                e_var = P.add("vector",
                              lambda v, varg=varg: v.tensor_single_scalar(
                                  varg, varg, float(EPS), ALU.add),
                              deps=[e_v1], inc=("VE", 1), small=True)
                e_ln = P.add("scalar",
                             lambda sc, varg=varg, lnvg=lnvg: sc.activation(
                                 lnvg, varg, AF.Ln),
                             deps=[e_var], inc=("ACT", 1), small=True)
                e_rs = P.add("scalar",
                             lambda sc, lnvg=lnvg, rstdg=rstdg: sc.activation(
                                 rstdg, lnvg, AF.Exp, scale=-0.5),
                             deps=[e_ln], inc=("ACT", 1), small=True)
                for r in range(gsz):
                    e_norm[(gi, r)] = P.add(
                        "vector",
                        lambda v, par=par, r=r, mg=mg, rstdg=rstdg: v.tensor_scalar(
                            lng[par][:, r * D:(r + 1) * D],
                            xg[par][:, r * D:(r + 1) * D],
                            mg[:, r:r + 1], rstdg[:, r:r + 1],
                            ALU.subtract, ALU.mult),
                        deps=[e_rs], inc=("VE", 1), small=True)
                last = None
                dst3 = dst_big[:, :].rearrange("p (dc t) -> p dc t", t=ntiles * 128)
                for r in range(gsz):
                    i = gi * gsz + r
                    for rr in range(4):
                        g4 = i * 4 + rr
                        bank = g4 % 2
                        e_tp = None
                        for k in range(4):
                            dc = rr * 4 + k
                            deps_tp = [e_norm[(gi, r)]]
                            if g4 >= 2:
                                deps_tp.append(e_ev[g4 - 2])
                            e_tp = P.add("tensor",
                                         lambda t, par=par, r=r, dc=dc, bank=bank,
                                         k=k: t.transpose(
                                             pb[bank][:, k * 128:(k + 1) * 128],
                                             lng[par][:, r * D + dc * 128:
                                                      r * D + (dc + 1) * 128],
                                             identb[:, :]),
                                         deps=(deps_tp if k == 0 else
                                               [e_ev.get(g4 - 2)]),
                                         inc=("PE", 1))
                        e_ev[g4] = P.add("vector",
                                         lambda v, i=i, rr=rr, bank=bank,
                                         dst3=dst3: v.tensor_copy(
                                             dst3[:, rr * 4:(rr + 1) * 4,
                                                  i * 128:(i + 1) * 128],
                                             pb[bank][:, 0:512].rearrange(
                                                 "p (n t) -> p n t", t=128)),
                                         deps=[e_tp], inc=("VE", 1))
                        last = e_ev[g4]
                e_grp_done[gi] = last
            return e_ev

        def phase_proj(wsrc, wbuf, src_T, nc_tiles, stage, dst_scr, ps, gate,
                       wgate=None):
            e_w = [None] * NDC
            for dc in range(NDC):
                e_w[dc] = P.add("gpsimd",
                                lambda g, dc=dc: g.dma_start(
                                    wbuf[dc][:, :], wsrc[dc * 128:(dc + 1) * 128, :]),
                                deps=((wgate if wgate is not None else gate)
                                      if dc == 0 else ()),
                                inc=("W", 16), dma=True)
            e_evac = {}
            e_dma = {}
            nstage = len(stage)
            hw = stage[0].shape[1] // 512
            nh = nc_tiles // hw
            for fc in range(NDC):
                for ncb in range(nc_tiles):
                    k = fc * nc_tiles + ncb
                    par = k % 2
                    kbuf = fc * nh + ncb // hw
                    deps0 = [e_w[NDC - 1]] + list(gate)
                    if k >= 2:
                        deps0.append(e_evac[k - 2])
                    for dc in range(NDC):
                        dd = list(deps0) if dc == 0 else []
                        e = P.add("tensor",
                                  lambda t, fc=fc, ncb=ncb, dc=dc, par=par: t.matmul(
                                      ps[par][:, :],
                                      wbuf[dc][:, fc * 128:(fc + 1) * 128],
                                      src_T[dc][:, ncb * 512:(ncb + 1) * 512],
                                      start=(dc == 0), stop=(dc == NDC - 1)),
                                  deps=dd,
                                  inc=(("PE", 1) if dc == NDC - 1 else None))
                    sdeps = [e, e_dma.get(kbuf - nstage)]
                    e_evac[k] = P.add("vector",
                                      lambda v, ncb=ncb, par=par, kbuf=kbuf,
                                      hw=hw: v.tensor_copy(
                                          stage[kbuf % nstage][:, (ncb % hw) * 512:
                                                               (ncb % hw + 1) * 512],
                                          ps[par][:, :]),
                                      deps=sdeps, inc=("VE", 1))
                    if ncb % hw == hw - 1:
                        e_dma[kbuf] = P.add(
                            "sync",
                            lambda sy, fc=fc, ncb=ncb, kbuf=kbuf, hw=hw: sy.dma_start(
                                dst_scr[fc * 128:(fc + 1) * 128,
                                        (ncb - hw + 1) * 512:(ncb + 1) * 512],
                                stage[kbuf % nstage][:, :]),
                            deps=[e_evac[fc * nc_tiles + ncb]],
                            inc=("SC", 16), dma=True)
            return e_dma

        join_prev = gate0
        P.phase = "A1"
        if "A1" in active:
            with ExitStack() as sc1:
                xolT_big = sc1.enter_context(
                    nc.sbuf_tensor("xolT", [128, NDC * NTOK], BF16))
                xolT = [xolT_big[:, i * NTOK:(i + 1) * NTOK] for i in range(NDC)]
                wq_bf = [sc1.enter_context(nc.sbuf_tensor(f"wq{i}", [128, D], BF16))
                         for i in range(NDC)]
                xbuf = [sc1.enter_context(nc.sbuf_tensor(f"xa{i}", [128, 2 * D], F32))
                        for i in range(2)]
                lnbuf = [sc1.enter_context(nc.sbuf_tensor(f"lna{i}", [128, 2 * D], BF16))
                         for i in range(2)]
                qstage = [sc1.enter_context(nc.sbuf_tensor(f"qst{i}", [128, NTOK], BF16))
                          for i in range(2)]
                psA = psum_set([("pbA0", [128, 1024], BF16), ("pbA1", [128, 1024], BF16),
                                ("psq0", [128, 512], F32), ("psq1", [128, 512], F32)])
                pb, psq = psA[0:2], psA[2:4]

                phase_ln_transpose(x_own, NTOK // 128, xolT, xolT_big, xbuf, lnbuf, pb, join_prev, 2)
                join_prev = P.join()
                if "xol_dump" in dump_scr:
                    xd = nc.dram_tensor("xol_dump", [D, NTOK], BF16,
                                        kind="ExternalOutput").ap()
                    for dc in range(NDC):
                        P.add("sync",
                              lambda sy, dc=dc, xd=xd: sy.dma_start(
                                  xd[dc * 128:(dc + 1) * 128, :], xolT[dc][:, :]),
                              deps=join_prev, inc=("OUT", 16), dma=True)
                if "BQ" in active:
                    P.phase = "BQ"
                    phase_proj(wqT, wq_bf, xolT, NTOK // 512, qstage, q_scr, psq,
                               join_prev, wgate=gate0)
                    join_prev = P.join()

        if "A2" in active:
            P.phase = "A2"
            sc2 = ExitStack()
            xln1T_big = sc2.enter_context(
                nc.sbuf_tensor("x1T", [128, NDC * T], BF16))
            xln1T = [xln1T_big[:, i * T:(i + 1) * T] for i in range(NDC)]
            with ExitStack() as sc2a:
                xbuf = [sc2a.enter_context(nc.sbuf_tensor(f"xb{i}", [128, 2 * D], F32))
                        for i in range(2)]
                lnbuf = [sc2a.enter_context(nc.sbuf_tensor(f"lnb{i}", [128, 2 * D], BF16))
                         for i in range(2)]
                pb = psum_set([("pbB0", [128, 1024], BF16), ("pbB1", [128, 1024], BF16)])
                phase_ln_transpose(x_seq, T // 128, xln1T, xln1T_big, xbuf, lnbuf, pb, join_prev, 2)
                join_prev = P.join()

            if "BK" in active:
                P.phase = "BK"
                with ExitStack() as sc2b:
                    wk_bf = [sc2b.enter_context(nc.sbuf_tensor(f"wk{i}", [128, D], BF16))
                             for i in range(NDC)]
                    kstage = [sc2b.enter_context(nc.sbuf_tensor(f"kst{i}", [128, T // 2], BF16))
                              for i in range(2)]
                    psk = psum_set([("psk0", [128, 512], F32), ("psk1", [128, 512], F32)])
                    phase_proj(wkT, wk_bf, xln1T, T // 512, kstage, k_scr, psk, join_prev)
                    join_prev = P.join()

            if "BV" in active:
                P.phase = "BV"
                with ExitStack() as sc2c:
                    wv_bf = [sc2c.enter_context(nc.sbuf_tensor(f"wv{i}", [128, D], BF16))
                             for i in range(NDC)]
                    psv = psum_set([(f"psv{i}", [128, 512], F32) for i in range(8)])
                    e_w = [None] * NDC
                    for dc in range(NDC):
                        e_w[dc] = P.add("gpsimd",
                                        lambda g, dc=dc: g.dma_start(
                                            wv_bf[dc][:, :],
                                            wvT[dc * 128:(dc + 1) * 128, :]),
                                        deps=(join_prev if dc == 0 else ()),
                                        inc=("W", 16), dma=True)
                    e_evl = {}
                    for ct in range(T // 128):
                        par = ct % 2
                        e_stop = None
                        for nf in range(4):
                            deps0 = [e_w[NDC - 1]] + list(join_prev)
                            if ct >= 2:
                                deps0 += list(e_evl[ct - 2])
                            for dc in range(NDC):
                                dd = list(deps0) if dc == 0 else []
                                e_stop = P.add(
                                    "tensor",
                                    lambda t, ct=ct, nf=nf, dc=dc, par=par: t.matmul(
                                        psv[4 * par + nf][:, :],
                                        xln1T[dc][:, ct * 128:(ct + 1) * 128],
                                        wv_bf[dc][:, nf * 512:(nf + 1) * 512],
                                        start=(dc == 0), stop=(dc == NDC - 1)),
                                    deps=dd,
                                    inc=(("PE", 1) if dc == NDC - 1 else None))
                        last_ve = last_ac = None
                        for dc in range(NDC):
                            if dc < 8:
                                last_ve = P.add(
                                    "vector",
                                    lambda v, ct=ct, dc=dc, par=par: v.tensor_copy(
                                        xln1T[dc][:, ct * 128:(ct + 1) * 128],
                                        psv[4 * par + dc // 4][:, (dc % 4) * 128:
                                                               (dc % 4 + 1) * 128]),
                                    deps=[e_stop], inc=("VE", 1))
                            else:
                                last_ac = P.add(
                                    "scalar",
                                    lambda s, ct=ct, dc=dc, par=par: s.copy(
                                        xln1T[dc][:, ct * 128:(ct + 1) * 128],
                                        psv[4 * par + dc // 4][:, (dc % 4) * 128:
                                                               (dc % 4 + 1) * 128]),
                                    deps=[e_stop], inc=("ACT", 1))
                        e_evl[ct] = (last_ve, last_ac)
                    join_prev = P.join()
                    if "v_dump" in dump_scr:
                        vd = nc.dram_tensor("v_dump", [D, T], BF16,
                                            kind="ExternalOutput").ap()
                        for dc in range(NDC):
                            P.add("sync",
                                  lambda sy, dc=dc, vd=vd: sy.dma_start(
                                      vd[dc * 128:(dc + 1) * 128, :], xln1T[dc][:, :]),
                                  deps=join_prev, inc=("OUT", 16), dma=True)

            if "C" in active:
                P.phase = "C"
                with ExitStack() as scC:
                    khb = [scC.enter_context(nc.sbuf_tensor(f"kh{i}", [128, T], BF16))
                           for i in range(2)]
                    qhb = [scC.enter_context(nc.sbuf_tensor(f"qh{i}", [128, 512], BF16))
                           for i in range(2)]
                    mkb = [scC.enter_context(nc.sbuf_tensor(f"mk{i}", [128, 4096], BF16))
                           for i in range(2)]
                    pbuf = [scC.enter_context(nc.sbuf_tensor(f"pp{i}", [128, 1024], BF16))
                            for i in range(8)]
                    zsb = [scC.enter_context(nc.sbuf_tensor(f"zs{i}", [128, 512], BF16))
                           for i in range(2)]
                    zstage = [scC.enter_context(nc.sbuf_tensor(f"zt{i}", [128, 512], BF16))
                              for i in range(2)]
                    rsb = [scC.enter_context(nc.sbuf_tensor(f"rs{i}", [128, 4], F32))
                           for i in range(2)]
                    psC = psum_set([("psS0", [128, 1024], F32), ("psS1", [128, 1024], F32),
                                    ("psZ0", [128, 512], F32), ("psZ1", [128, 512], F32),
                                    ("psSum", [128, 4], F32),
                                    ("psZtp", [128, 1024], BF16)])
                    psS = psC[0:2]
                    psZ = psC[2:4]
                    psSum = psC[4]
                    psZtp = psC[5]

                    psS_last = {0: None, 1: None}
                    pbuf_last = {i: None for i in range(8)}
                    e_mk = {}
                    e_mk_lastuse = {}
                    e_scale = {}
                    e_recip = {}
                    e_zev = {}
                    e_zdma = {}
                    e_avl = {}
                    hseq = [(j, h) for j in range(4) for h in range(NH)]

                    def emit_tail(j, h, hi, par, e2):
                        e_recip[(j, h)] = P.add("vector",
                                                lambda v, par=par: v.reciprocal(
                                                    rsb[par][:, :], psSum[:, :]),
                                                deps=[e2], inc=("VE", 1), small=True)
                        e_sc = None
                        for qs in range(4):
                            e_sc = P.add("vector",
                                         lambda v, par=par, qs=qs: v.tensor_scalar(
                                             zsb[par][:, qs * 128:(qs + 1) * 128],
                                             psZ[par][:, qs * 128:(qs + 1) * 128],
                                             rsb[par][:, qs:qs + 1], None, ALU.mult),
                                         deps=[e_recip[(j, h)]], inc=("VE", 1))
                        e_scale[(j, h)] = e_sc
                        e_tp = None
                        for qs in range(4):
                            deps_t = [e_sc]
                            if hi >= 1:
                                deps_t.append(e_zev[hseq[hi - 1]])
                            e_tp = P.add("tensor",
                                         lambda t, par=par, qs=qs: t.transpose(
                                             psZtp[:, qs * 128:(qs + 1) * 128],
                                             zsb[par][:, qs * 128:(qs + 1) * 128],
                                             identb[:, :]),
                                         deps=(deps_t if qs == 0 else ()), inc=("PE", 1))
                        deps_z = [e_tp]
                        if hi >= 2:
                            deps_z.append(e_zdma[hseq[hi - 2]])
                        e_zev[(j, h)] = P.add("vector",
                                              lambda v, par=par: v.tensor_copy(
                                                  zstage[par][:, :], psZtp[:, 0:512]),
                                              deps=deps_z, inc=("VE", 1))
                        e_zdma[(j, h)] = P.add("sync",
                                               lambda sy, j=j, h=h, par=par: sy.dma_start(
                                                   z_scr[h * 128:(h + 1) * 128,
                                                         j * 512:(j + 1) * 512],
                                                   zstage[par][:, :]),
                                               deps=[e_zev[(j, h)]], inc=("SC", 16),
                                               dma=True)

                    def emit_avsums(j, h, hi, par, grp, spar, dep_av, nsub):
                        e2 = None
                        for s2 in range(2):
                            c = 2 * grp + s2
                            for qs in range(4):
                                first = (c == 0 and qs == 0)
                                lastt = (c == nsub - 1 and qs == 3)
                                deps_av = []
                                if s2 == 0 and qs == 0:
                                    deps_av.append(dep_av)
                                if first:
                                    if hi >= 2:
                                        deps_av.append(e_scale[hseq[hi - 2]])
                                    if hi >= 1:
                                        deps_av.append(e_recip[hseq[hi - 1]])
                                lo = s2 * 512 + qs * 128
                                P.add("tensor",
                                      lambda t, h=h, c=c, par=par, spar=spar,
                                      qs=qs, lo=lo, first=first, lastt=lastt: t.matmul(
                                          psZ[par][:, qs * 128:(qs + 1) * 128],
                                          pbuf[spar][:, lo:lo + 128],
                                          xln1T[h][:, c * 128:(c + 1) * 128],
                                          start=first, stop=lastt),
                                      deps=deps_av, inc=("PE", 1))
                                e2 = P.add("tensor",
                                           lambda t, spar=spar, qs=qs, lo=lo,
                                           first=first, lastt=lastt: t.matmul(
                                               psSum[:, qs:qs + 1],
                                               pbuf[spar][:, lo:lo + 128],
                                               onescol[:, :],
                                               start=first, stop=lastt),
                                           deps=(), inc=("PE", 1))
                        pbuf_last[spar] = e2
                        return e2

                    pending_tail = None
                    for hi, (j, h) in enumerate(hseq):
                        par = hi % 2
                        nsub = 4 * CAP[j]
                        ngrp = nsub // 2
                        deps_k = list(join_prev)
                        if hi >= 2:
                            deps_k.append(e_avl[hseq[hi - 2]])
                        e_k = P.add("sync",
                                    lambda sy, h=h, par=par, nsub=nsub: sy.dma_start(
                                        khb[par][:, 0:nsub * 128],
                                        k_scr[h * 128:(h + 1) * 128, 0:nsub * 128]),
                                    deps=deps_k, inc=("SR", 16), dma=True)
                        e_q = P.add("sync",
                                    lambda sy, j=j, h=h, par=par: sy.dma_start(
                                        qhb[par][:, :],
                                        q_scr[h * 128:(h + 1) * 128, j * 512:(j + 1) * 512]),
                                    deps=deps_k, inc=("SR", 16), dma=True)
                        if h == 0:
                            deps_m = list(join_prev)
                            if j >= 2:
                                deps_m.append(e_mk_lastuse[j - 2])
                            e_mk[j] = P.add(
                                "sync",
                                lambda sy, j=j: sy.dma_start(
                                    mkb[j % 2][:, :].rearrange("p (g q) -> p g q", q=1024),
                                    masks[j].rearrange("g p q -> p g q")),
                                deps=deps_m, inc=("SR", 16), dma=True)
                        pend = None
                        for grp in range(ngrp):
                            spar = grp % 2
                            ppar = (hi * 20 + grp) % 8
                            masked = grp >= ngrp - 4
                            gi = grp - (ngrp - 4)
                            deps_s = [e_k, e_q, psS_last[spar]]
                            for s2 in range(2):
                                c = 2 * grp + s2
                                e_sg = P.add("tensor",
                                             lambda t, c=c, s2=s2, spar=spar, par=par: t.matmul(
                                                 psS[spar][:, s2 * 512:(s2 + 1) * 512],
                                                 khb[par][:, c * 128:(c + 1) * 128],
                                                 qhb[par][:, :],
                                                 start=True, stop=True),
                                             deps=(deps_s if s2 == 0 else ()),
                                             inc=("PE", 1))
                            e_exp = P.add("scalar",
                                          lambda s, spar=spar, ppar=ppar: s.activation(
                                              pbuf[ppar][:, :], psS[spar][:, :],
                                              AF.Exp, scale=SCALE),
                                          deps=[e_sg, pbuf_last[ppar]], inc=("ACT", 1))
                            psS_last[spar] = e_exp
                            dep_av = e_exp
                            if masked:
                                dep_av = P.add("vector",
                                               lambda v, j=j, gi=gi, ppar=ppar: v.tensor_mul(
                                                   pbuf[ppar][:, :], pbuf[ppar][:, :],
                                                   mkb[j % 2][:, gi * 1024:(gi + 1) * 1024]),
                                               deps=[e_exp, e_mk[j]], inc=("VE", 1))
                                e_mk_lastuse[j] = dep_av
                            if grp == 0 and pending_tail is not None:
                                pending_tail()
                                pending_tail = None
                            if pend is not None:
                                emit_avsums(*pend)
                            pend = (j, h, hi, par, grp, ppar, dep_av, nsub)
                        e2 = emit_avsums(*pend)
                        e_avl[(j, h)] = e2
                        pending_tail = (lambda j=j, h=h, hi=hi, par=par, e2=e2:
                                        emit_tail(j, h, hi, par, e2))
                    pending_tail()
                    join_prev = P.join()
            sc2.close()

        if "D" in active:
            P.phase = "D"
            with ExitStack() as scD:
                zT = [scD.enter_context(nc.sbuf_tensor(f"zT{i}", [128, NTOK], BF16))
                      for i in range(NDC)]
                wo_bf = [scD.enter_context(nc.sbuf_tensor(f"wo{i}", [128, D], BF16))
                         for i in range(NDC)]
                xob = [scD.enter_context(nc.sbuf_tensor(f"xo{i}", [128, D], F32))
                       for i in range(2)]
                t1b = scD.enter_context(nc.sbuf_tensor("t1b", [128, D], F32))
                x1b = [scD.enter_context(nc.sbuf_tensor(f"x1b{i}", [128, D], F32))
                       for i in range(2)]
                xl2b = [scD.enter_context(nc.sbuf_tensor(f"xl2b{i}", [128, D], BF16))
                        for i in range(2)]
                pso = psum_set([("psoA", [128, 2048], F32), ("psoB", [128, 2048], F32)])

                e_z = [None] * NDC
                e_w = [None] * NDC
                for fc in range(NDC):
                    e_z[fc] = P.add("sync",
                                    lambda sy, fc=fc: sy.dma_start(
                                        zT[fc][:, :], z_scr[fc * 128:(fc + 1) * 128, :]),
                                    deps=(join_prev if fc == 0 else ()),
                                    inc=("SR", 16), dma=True)
                    e_w[fc] = P.add("gpsimd",
                                    lambda g, fc=fc: g.dma_start(
                                        wo_bf[fc][:, :], woT[fc * 128:(fc + 1) * 128, :]),
                                    deps=(join_prev if fc == 0 else ()),
                                    inc=("W", 16), dma=True)
                e_res = {}
                e_x1dma = {}
                e_xl2dma = {}
                for m in range(NTOK // 128):
                    par = m % 2
                    e_xo = P.add("sync",
                                 lambda sy, m=m, par=par: sy.dma_start(
                                     xob[par][:, :], x_own[m * 128:(m + 1) * 128, :]),
                                 deps=([e_res[m - 2]] if m >= 2 else list(join_prev)),
                                 inc=("IN", 16), dma=True)
                    e_stop = {}
                    for nf in range(4):
                        deps0 = [e_z[NDC - 1], e_w[NDC - 1]]
                        if m >= 2:
                            deps0.append(e_res[m - 2])
                        for fc in range(NDC):
                            e = P.add("tensor",
                                      lambda t, m=m, nf=nf, fc=fc, par=par: t.matmul(
                                          pso[par][:, nf * 512:(nf + 1) * 512],
                                          zT[fc][:, m * 128:(m + 1) * 128],
                                          wo_bf[fc][:, nf * 512:(nf + 1) * 512],
                                          start=(fc == 0), stop=(fc == NDC - 1)),
                                      deps=(deps0 if fc == 0 else ()),
                                      inc=(("PE", 1) if fc == NDC - 1 else None))
                        e_stop[nf] = e
                    regions = [(pso[par][:, nf * 512:(nf + 1) * 512], e_stop[nf])
                               for nf in range(4)]
                    m_ap, r_ap, e_rs = ln_stats_events(regions, par)
                    e_n = P.add("vector",
                                lambda v, par=par, m_ap=m_ap,
                                r_ap=r_ap: v.tensor_scalar(
                                    t1b[:, :], pso[par][:, :], m_ap, r_ap,
                                    ALU.subtract, ALU.mult),
                                deps=[e_rs], inc=("VE", 1))
                    deps_a = [e_n, e_xo]
                    if m >= 2:
                        deps_a.append(e_x1dma[m - 2])
                    e_add = P.add("vector",
                                  lambda v, par=par: v.tensor_add(
                                      x1b[par][:, :], t1b[:, :], xob[par][:, :]),
                                  deps=deps_a, inc=("VE", 1))
                    e_x1dma[m] = P.add("sync",
                                       lambda sy, m=m, par=par: sy.dma_start(
                                           x1_scr[m * 128:(m + 1) * 128, :],
                                           x1b[par][:, :]),
                                       deps=[e_add], inc=("SC", 16), dma=True)
                    m2_ap, r2_ap, e_rs2 = ln_stats_events([(x1b[par][:, :], e_add)], par)
                    deps_r = [e_rs2]
                    if m >= 2:
                        deps_r.append(e_xl2dma[m - 2])
                    e_res[m] = P.add("vector",
                                     lambda v, par=par, m2=m2_ap, r2=r2_ap: v.tensor_scalar(
                                         xl2b[par][:, :], x1b[par][:, :], m2, r2,
                                         ALU.subtract, ALU.mult),
                                     deps=deps_r, inc=("VE", 1))
                    e_xl2dma[m] = P.add("sync",
                                        lambda sy, m=m, par=par: sy.dma_start(
                                            xln2_scr[m * 128:(m + 1) * 128, :],
                                            xl2b[par][:, :]),
                                        deps=[e_res[m]], inc=("SC", 16), dma=True)
                join_prev = P.join()

        if "F0" in active:
            P.phase = "F0"
            scF = ExitStack()
            xl2T_big = scF.enter_context(
                nc.sbuf_tensor("x2T", [128, NDC * NTOK], BF16))
            xl2T = [xl2T_big[:, i * NTOK:(i + 1) * NTOK] for i in range(NDC)]
            with ExitStack() as scF0:
                xtile = [scF0.enter_context(nc.sbuf_tensor(f"f0x{i}", [128, D], BF16))
                         for i in range(4)]
                pb = psum_set([("pbF0", [128, 1024], BF16), ("pbF1", [128, 1024], BF16)])
                e_ev = {}
                for i in range(NTOK // 128):
                    par = i % 4
                    deps = list(join_prev)
                    if i >= 4:
                        deps.append(e_ev[(i - 4) * NDC + NDC - 1])
                    e_ld = P.add("sync",
                                 lambda sy, i=i, par=par: sy.dma_start(
                                     xtile[par][:, :],
                                     xln2_scr[i * 128:(i + 1) * 128, :]),
                                 deps=deps, inc=("SR", 16), dma=True)
                    for dc in range(NDC):
                        g = i * NDC + dc
                        bank, sl = g % 2, (g // 2) % 8
                        deps_tp = [e_ld]
                        if g >= 2:
                            deps_tp.append(e_ev[g - 2])
                        e_tp = P.add("tensor",
                                     lambda t, par=par, dc=dc, bank=bank, sl=sl: t.transpose(
                                         pb[bank][:, sl * 128:(sl + 1) * 128],
                                         xtile[par][:, dc * 128:(dc + 1) * 128],
                                         identb[:, :]),
                                     deps=deps_tp, inc=("PE", 1))
                        e_ev[g] = P.add("vector",
                                        lambda v, i=i, dc=dc, bank=bank, sl=sl: v.tensor_copy(
                                            xl2T[dc][:, i * 128:(i + 1) * 128],
                                            pb[bank][:, sl * 128:(sl + 1) * 128]),
                                        deps=[e_tp], inc=("VE", 1))
                join_prev = P.join()

            if "F1" in active:
                P.phase = "F1"
                with ExitStack() as scF1:
                    wgb = [scF1.enter_context(nc.sbuf_tensor(f"wgb{i}", [128, D], BF16))
                           for i in range(8)]
                    wub = [scF1.enter_context(nc.sbuf_tensor(f"wub{i}", [128, D], BF16))
                           for i in range(8)]
                    gbuf = [scF1.enter_context(nc.sbuf_tensor(f"gbuf{i}", [128, 512], BF16))
                            for i in range(2)]
                    hbuf = [scF1.enter_context(nc.sbuf_tensor(f"hb{i}", [128, 2048], BF16))
                            for i in range(4)]
                    abuf = [scF1.enter_context(nc.sbuf_tensor(f"ab{i}", [128, 512], BF16))
                            for i in range(4)]
                    bbuf = [scF1.enter_context(nc.sbuf_tensor(f"bb{i}", [128, 512], BF16))
                            for i in range(4)]
                    cbuf = scF1.enter_context(nc.sbuf_tensor("cb", [128, 512], BF16))
                    hsp = [scF1.enter_context(nc.sbuf_tensor(f"hsp{i}", [128, 2048], BF16))
                           for i in range(4)]
                    wdst = [scF1.enter_context(nc.sbuf_tensor("wdst0", [128, 8192], BF16))]
                    psF = psum_set([("psg0", [128, 512], F32), ("psg1", [128, 512], F32),
                                    ("psu0", [128, 512], F32), ("psu1", [128, 512], F32)])

                    e_hdma = {}
                    e_out_prev = {}
                    e_ge_prev = {}
                    e_hm = {}
                    e_wu_stop = {}
                    e_wdcast = {}
                    e_wdwrite = {}
                    for gc in range(16):
                        gpar = gc % 2
                        e_wdcast[gc] = P.add(
                            "gpsimd",
                            lambda g, gc=gc, gpar=gpar: g.dma_start(
                                wdst[0][:, :].rearrange("a (ic b) -> a ic b", b=128),
                                wd_t[gc].rearrange("ic a b -> a ic b")),
                            deps=([e_wdwrite[gc - 1]] if gc >= 1 else list(join_prev)),
                            inc=("W", 16), dma=True)
                        e_wdwrite[gc] = P.add(
                            "sync",
                            lambda sy, gc=gc, gpar=gpar: sy.dma_start(
                                wd_bf_scr[gc].rearrange("ic a b -> a ic b"),
                                wdst[0][:, :].rearrange("a (ic b) -> a ic b", b=128)),
                            deps=[e_wdcast[gc]], inc=("SC", 16), dma=True)
                        e_wg = {}
                        e_wu = {}
                        for j in range(4):
                            deps_w = list(join_prev)
                            if gc >= 2:
                                deps_w.append(e_wu_stop[(gc - 2, j)])
                            e_wg[j] = P.add("gpsimd",
                                            lambda g, j=j, gc=gc, gpar=gpar: g.dma_start(
                                                wgb[4 * gpar + j][:, :].rearrange(
                                                    "a (dc b) -> a dc b", b=128),
                                                wg_t[j, gc].rearrange("dc a b -> a dc b")),
                                            deps=deps_w, inc=("W", 16), dma=True)
                            e_wu[j] = P.add("gpsimd",
                                            lambda g, j=j, gc=gc, gpar=gpar: g.dma_start(
                                                wub[4 * gpar + j][:, :].rearrange(
                                                    "a (dc b) -> a dc b", b=128),
                                                wu_t[j, gc].rearrange("dc a b -> a dc b")),
                                            deps=deps_w, inc=("W", 16), dma=True)
                        for tN in range(4):
                            for j in range(4):
                                idx = (gc * 4 + tN) * 4 + j
                                ppar = idx % 2
                                deps_g = [e_wg[j], e_hm.get(idx - 2)]
                                for dc in range(NDC):
                                    e_g = P.add("tensor",
                                                lambda t, dc=dc, tN=tN, j=j, gpar=gpar,
                                                ppar=ppar: t.matmul(
                                                    psF[ppar][:, :],
                                                    wgb[4 * gpar + j][:, dc * 128:(dc + 1) * 128],
                                                    xl2T[dc][:, tN * 512:(tN + 1) * 512],
                                                    start=(dc == 0), stop=(dc == NDC - 1)),
                                                deps=(deps_g if dc == 0 else ()),
                                                inc=(("PE", 1) if dc == NDC - 1 else None))
                                e_silu = P.add("scalar",
                                               lambda s, ppar=ppar: s.activation(
                                                   gbuf[ppar][:, :], psF[ppar][:, :],
                                                   AF.Silu),
                                               deps=[e_g, e_hm.get(idx - 2)],
                                               inc=("ACT", 1))
                                deps_u = [e_wu[j], e_hm.get(idx - 2)]
                                for dc in range(NDC):
                                    e_u = P.add("tensor",
                                                lambda t, dc=dc, tN=tN, j=j, gpar=gpar,
                                                ppar=ppar: t.matmul(
                                                    psF[2 + ppar][:, :],
                                                    wub[4 * gpar + j][:, dc * 128:(dc + 1) * 128],
                                                    xl2T[dc][:, tN * 512:(tN + 1) * 512],
                                                    start=(dc == 0), stop=(dc == NDC - 1)),
                                                deps=(deps_u if dc == 0 else ()),
                                                inc=(("PE", 1) if dc == NDC - 1 else None))
                                e_wu_stop[(gc, j)] = e_u
                                deps_hm = [e_u, e_silu, e_out_prev.get((tN, j))]
                                e_hm[idx] = P.add("vector",
                                                  lambda v, j=j, tN=tN, ppar=ppar: v.tensor_mul(
                                                      hbuf[j][:, tN * 512:(tN + 1) * 512],
                                                      psF[2 + ppar][:, :], gbuf[ppar][:, :]),
                                                  deps=deps_hm, inc=("VE", 1))
                            sl = (tN * 512, (tN + 1) * 512)
                            e_abs = []
                            for jj in range(4):
                                e_abs.append(P.add("scalar",
                                                   lambda s, jj=jj, sl=sl: s.activation(
                                                       abuf[jj][:, :],
                                                       hbuf[jj][:, sl[0]:sl[1]], AF.Abs),
                                                   deps=[e_hm[(gc * 4 + tN) * 4 + jj],
                                                         e_ge_prev.get(jj)],
                                                   inc=("ACT", 1)))
                            em1 = P.add("vector",
                                        lambda v: v.tensor_max(bbuf[0][:, :], abuf[0][:, :],
                                                               abuf[1][:, :]),
                                        deps=[e_abs[0], e_abs[1]], inc=("VE", 1))
                            en1 = P.add("vector",
                                        lambda v: v.tensor_tensor(bbuf[1][:, :],
                                                                  abuf[0][:, :],
                                                                  abuf[1][:, :], ALU.min),
                                        deps=[e_abs[1]], inc=("VE", 1))
                            em2 = P.add("vector",
                                        lambda v: v.tensor_max(bbuf[2][:, :], abuf[2][:, :],
                                                               abuf[3][:, :]),
                                        deps=[e_abs[2], e_abs[3]], inc=("VE", 1))
                            en2 = P.add("vector",
                                        lambda v: v.tensor_tensor(bbuf[3][:, :],
                                                                  abuf[2][:, :],
                                                                  abuf[3][:, :], ALU.min),
                                        deps=[e_abs[3]], inc=("VE", 1))
                            es = P.add("vector",
                                       lambda v: v.tensor_tensor(cbuf[:, :], bbuf[0][:, :],
                                                                 bbuf[2][:, :], ALU.min),
                                       deps=[em2], inc=("VE", 1))
                            et = P.add("vector",
                                       lambda v: v.tensor_max(bbuf[1][:, :], bbuf[1][:, :],
                                                              bbuf[3][:, :]),
                                       deps=[en2], inc=("VE", 1))
                            ethr = P.add("vector",
                                         lambda v: v.tensor_max(cbuf[:, :], cbuf[:, :],
                                                                bbuf[1][:, :]),
                                         deps=[es, et], inc=("VE", 1))
                            for jj in range(4):
                                ege = P.add("vector",
                                            lambda v, jj=jj: v.tensor_tensor(
                                                bbuf[jj][:, :], abuf[jj][:, :],
                                                cbuf[:, :], ALU.is_ge),
                                            deps=[ethr], inc=("VE", 1))
                                e_ge_prev[jj] = ege
                                e_out_prev[(tN, jj)] = P.add(
                                    "vector",
                                    lambda v, jj=jj, sl=sl: v.tensor_mul(
                                        hsp[jj][:, sl[0]:sl[1]],
                                        hbuf[jj][:, sl[0]:sl[1]], bbuf[jj][:, :]),
                                    deps=[ege, e_hdma.get((gc - 1, jj))],
                                    inc=("VE", 1))
                        for j in range(4):
                            deps_h = [e_out_prev[(tN, j)] for tN in range(4)]
                            e_hdma[(gc, j)] = P.add(
                                "sync",
                                lambda sy, gc=gc, j=j: sy.dma_start(
                                    hs_scr[:, j * 2048 + gc * 128:
                                           j * 2048 + (gc + 1) * 128,
                                           :].rearrange("tq p t -> p tq t"),
                                    hsp[j][:, :].rearrange("p (tq t) -> p tq t", t=512)),
                                deps=deps_h, inc=("SC", 16), dma=True)
                    join_prev = P.join()
            scF.close()

        e_outdma = {}
        P.phase = "F2"
        if "o2_dump" in dump_scr:
            o2_dump = nc.dram_tensor("o2_dump", [NTOK, D], F32,
                                     kind="ExternalOutput").ap()
        if "F2" in active:
            with ExitStack() as scG:
                hq = scG.enter_context(nc.sbuf_tensor("hq", [128, 64 * 512], BF16))
                wdb = [scG.enter_context(nc.sbuf_tensor(f"wdb{i}", [128, 8192], BF16))
                       for i in range(2)]
                dnst = [scG.enter_context(nc.sbuf_tensor(f"dnst{i}", [128, 512], F32))
                        for i in range(2)]
                o2tm = [scG.enter_context(nc.sbuf_tensor(f"o2tm{i}", [128, D], F32))
                        for i in range(4)]
                x1t = [scG.enter_context(nc.sbuf_tensor(f"x1t{i}", [128, D], F32))
                       for i in range(2)]
                outst = [scG.enter_context(nc.sbuf_tensor(f"outst{i}", [128, D], F32))
                         for i in range(2)]
                t2b = scG.enter_context(nc.sbuf_tensor("t2b", [128, D], F32))
                psd = psum_set([("psd0", [128, 512], F32), ("psd1", [128, 512], F32),
                                ("pst0", [128, 512], F32), ("pst1", [128, 512], F32)])

                e_fin = {}
                e_it_stop = {}
                e_tpe = {}
                e_nrm_prev = {}
                for tq in range(4):
                    deps_h = list(join_prev)
                    if tq >= 1:
                        deps_h.append(e_it_stop[tq * NDC - 1])
                    e_ha = P.add("sync",
                                 lambda sy, tq=tq: sy.dma_start(
                                     hq[:, 0:32 * 512].rearrange(
                                         "p (ic t) -> p ic t", t=512),
                                     hs_scr[tq, 0:32 * 128].rearrange(
                                         "(ic p) t -> p ic t", p=128)),
                                 deps=deps_h, inc=("SR", 16), dma=True)
                    e_hb = P.add("scalar",
                                 lambda sc, tq=tq: sc.dma_start(
                                     hq[:, 32 * 512:].rearrange(
                                         "p (ic t) -> p ic t", t=512),
                                     hs_scr[tq, 32 * 128:].rearrange(
                                         "(ic p) t -> p ic t", p=128)),
                                 deps=deps_h, inc=("SR", 16), dma=True)
                    e_h = e_hb
                    for dc in range(NDC):
                        it = tq * NDC + dc
                        par = it % 2
                        deps_w = list(join_prev)
                        if it >= 2:
                            deps_w.append(e_it_stop[it - 2])
                        e_wd2 = P.add("gpsimd",
                                      lambda g, dc=dc, par=par: g.dma_start(
                                          wdb[par][:, :].rearrange("a (ic b) -> a ic b", b=128),
                                          wd_bf_scr[dc].rearrange("ic a b -> a ic b")),
                                      deps=deps_w, inc=("W", 16), dma=True)
                        deps0 = [e_wd2, e_ha, e_hb]
                        if it >= 2:
                            deps0.append(e_tpe[(it - 2) * 4 + 3])
                        for ic in range(64):
                            e = P.add("tensor",
                                      lambda t, ic=ic, par=par: t.matmul(
                                          psd[par][:, :],
                                          wdb[par][:, ic * 128:(ic + 1) * 128],
                                          hq[:, ic * 512:(ic + 1) * 512],
                                          start=(ic == 0), stop=(ic == 63)),
                                      deps=(deps0 if ic == 0 else ()),
                                      inc=(("PE", 1) if ic == 63 else None))
                        e_it_stop[it] = e
                        e_dn = P.add("vector",
                                     lambda v, par=par: v.tensor_copy(
                                         dnst[par][:, :], psd[par][:, :]),
                                     deps=[e], inc=("VE", 1))
                        for mq in range(4):
                            tt = it * 4 + mq
                            bank = 2 + tt % 2
                            col = ((tt // 2) % 2) * 128
                            deps_tp = [e_dn, e_tpe.get(tt - 2),
                                       e_nrm_prev.get(mq) if dc == 0 else None]
                            e_tp = P.add("tensor",
                                         lambda t, bank=bank, col=col, par=par,
                                         mq=mq: t.transpose(
                                             psd[bank][:, col:col + 128],
                                             dnst[par][:, mq * 128:(mq + 1) * 128],
                                             identf[:, :]),
                                         deps=deps_tp, inc=("PE", 1))
                            e_tpe[tt] = P.add(
                                "vector",
                                lambda v, bank=bank, col=col, mq=mq, dc=dc: v.tensor_copy(
                                    o2tm[mq][:, dc * 128:(dc + 1) * 128],
                                    psd[bank][:, col:col + 128]),
                                deps=[e_tp], inc=("VE", 1))
                    for mq in range(4):
                        par = mq % 2
                        mglob = tq * 4 + mq
                        if "o2_dump" in dump_scr:
                            P.add("sync",
                                  lambda sy, mq=mq, mglob=mglob: sy.dma_start(
                                      o2_dump[mglob * 128:(mglob + 1) * 128, :],
                                      o2tm[mq][:, :]),
                                  deps=[e_tpe[(tq * NDC + NDC - 1) * 4 + mq]],
                                  inc=("OUT", 16), dma=True)
                        e_x1 = P.add("sync",
                                     lambda sy, mglob=mglob, par=par: sy.dma_start(
                                         x1t[par][:, :],
                                         x1_scr[mglob * 128:(mglob + 1) * 128, :]),
                                     deps=([e_fin[mglob - 2]] if mglob >= 2
                                           else list(join_prev)),
                                     inc=("SR", 16), dma=True)
                        dep_o2 = e_tpe[(tq * NDC + NDC - 1) * 4 + mq]
                        m3, r3, e_rs3 = ln_stats_events([(o2tm[mq][:, :], dep_o2)], par)
                        e_n = P.add("vector",
                                    lambda v, mq=mq, m3=m3, r3=r3: v.tensor_scalar(
                                        t2b[:, :], o2tm[mq][:, :], m3, r3,
                                        ALU.subtract, ALU.mult),
                                    deps=[e_rs3], inc=("VE", 1))
                        e_nrm_prev[mq] = e_n
                        deps_f = [e_n, e_x1, e_outdma.get(mglob - 2)]
                        e_fin[mglob] = P.add("vector",
                                             lambda v, par=par: v.tensor_add(
                                                 outst[par][:, :], t2b[:, :],
                                                 x1t[par][:, :]),
                                             deps=deps_f, inc=("VE", 1))
                        e_outdma[mglob] = P.add(
                            "sync",
                            lambda sy, mglob=mglob, par=par: sy.dma_start(
                                out[mglob * 128:(mglob + 1) * 128, :], outst[par][:, :]),
                            deps=[e_fin[mglob]], inc=("OUT", 16), dma=True)
                join_prev = P.join()

        out_total = sum(i[3][1] for i in P.events if i[3] and i[3][0] == "OUT")
        sc_total = sum(i[3][1] for i in P.events if i[3] and i[3][0] == "SC")

        def final_wait(sy):
            if out_total:
                sy.wait_ge(P.sems["OUT"], out_total)
            if sc_total:
                sy.wait_ge(P.sems["SC"], sc_total)
            return sy.nop()
        P.add("sync", final_wait, deps=(), inc=None)

        with nc.Block() as block:
            P.emit_all(block)

    return nc



def _host_layouts(Wq, Wk, Wv, Wo, w_gate, w_up, w_down):
    wqT = np.ascontiguousarray(Wq.T)
    wkT = np.ascontiguousarray(Wk.T)
    wvT = np.ascontiguousarray(Wv.T)
    woT = np.ascontiguousarray(Wo.T)

    def gu_tiles(w):
        r = w.reshape(D, 2048, 4)
        r = r.transpose(2, 0, 1)
        r = r.reshape(4, NDC, 128, 16, 128)
        r = r.transpose(0, 3, 1, 2, 4)
        return np.ascontiguousarray(r)
    wg_t = gu_tiles(w_gate)
    wu_t = gu_tiles(w_up)
    wdp = w_down.reshape(2048, 4, D).transpose(1, 0, 2).reshape(II, D)
    wd_t = np.ascontiguousarray(
        wdp.reshape(64, 128, NDC, 128).transpose(2, 0, 1, 3))
    return wqT, wkT, wvT, woT, wg_t, wu_t, wd_t


def _host_masks(p):
    G = GSETS[p]
    mk = np.zeros((4, 4, 128, 1024), dtype=np.float32)
    for j in range(4):
        g = G[j]
        for gi in range(4):
            for s2 in range(2):
                c = 8 * j + 2 * gi + s2
                keys = 128 * c + np.arange(128)[:, None]
                qs = 512 * g + np.arange(512)[None, :]
                mk[j, gi, :, s2 * 512:(s2 + 1) * 512] = (keys <= qs)
    return mk.astype(ml_dtypes.bfloat16)


def _make_in_maps(x, ws, mks):
    wqT, wkT, wvT, woT, wg_t, wu_t, wd_t = ws
    in_maps = []
    for c in range(8):
        b, p = divmod(c, 2)
        G = GSETS[p]
        x_own = np.concatenate([x[b, 512 * g:512 * (g + 1)] for g in G], axis=0)
        in_maps.append({
            "x_own": np.ascontiguousarray(x_own),
            "x_seq": np.ascontiguousarray(x[b]),
            "wqT": wqT, "wkT": wkT, "wvT": wvT, "woT": woT,
            "wg_t": wg_t, "wu_t": wu_t, "wd_t": wd_t,
            "masks": mks[p],
        })
    return in_maps


_NC_CACHE = {}


def _get_nc(key=("F2", ())):
    if key not in _NC_CACHE:
        _NC_CACHE[key] = build_nc(upto=key[0], dump_scr=key[1])
    return _NC_CACHE[key]


def kernel(L0_attn_x, L0_attn_Wq, L0_attn_Wk, L0_attn_Wv, L0_attn_Wo,
           L0_ffn_w_gate, L0_ffn_w_up, L0_ffn_w_down):
    x = np.asarray(L0_attn_x, dtype=np.float32)
    ws = _host_layouts(np.asarray(L0_attn_Wq, np.float32),
                       np.asarray(L0_attn_Wk, np.float32),
                       np.asarray(L0_attn_Wv, np.float32),
                       np.asarray(L0_attn_Wo, np.float32),
                       np.asarray(L0_ffn_w_gate, np.float32),
                       np.asarray(L0_ffn_w_up, np.float32),
                       np.asarray(L0_ffn_w_down, np.float32))
    mks = [_host_masks(0), _host_masks(1)]
    in_maps = _make_in_maps(x, ws, mks)
    nc = _get_nc()
    res = run_bass_kernel_spmd(nc, in_maps, core_ids=list(range(8)))
    y = np.empty((B, T, D), dtype=np.float32)
    for c in range(8):
        b, p = divmod(c, 2)
        G = GSETS[p]
        o = res.results[c]["out"]
        for j, g in enumerate(G):
            y[b, 512 * g:512 * (g + 1)] = o[512 * j:512 * (j + 1)]
    return y


# version 13
# speedup vs baseline: 1.0913x; 1.0913x over previous
import math
from contextlib import ExitStack

import numpy as np
import ml_dtypes

import concourse.bass as bass
import concourse.mybir as mybir
import concourse.bass_utils as _bass_utils
from concourse.bass_utils import run_bass_kernel_spmd

if not getattr(_bass_utils, "_ldw_opt_patched", False):
    _orig_run_command = _bass_utils.run_command

    def _run_command_ldw(argv, **kwargs):
        argv = ["--enable-ldw-opt=true" if a == "--enable-ldw-opt=false" else a
                for a in argv]
        return _orig_run_command(argv, **kwargs)

    _bass_utils.run_command = _run_command_ldw
    _bass_utils._ldw_opt_patched = True


F32 = mybir.dt.float32
BF16 = mybir.dt.bfloat16
AF = mybir.ActivationFunctionType
ALU = mybir.AluOpType
AX = mybir.AxisListType

D = 2048
T = 4096
B = 4
NH = 16
II = 8192
NTOK = 2048
NDC = 16
CAP = [2, 4, 6, 8]
GSETS = [[0, 3, 4, 7], [1, 2, 5, 6]]
SCALE = 1.0 / math.sqrt(128.0)
EPS = 1e-5
INV_D = 1.0 / 2048.0

PHASES = ["A1", "BQ", "A2", "BK", "BV", "C", "D", "F0", "F1", "F2"]


class Plan:

    def __init__(self, nc):
        self.nc = nc
        self.events = []
        self.sems = {}
        self.last_inc_by_engine = {}
        self.phase = "init"
        self.phases = []

    def sem(self, name):
        if name not in self.sems:
            self.sems[name] = self.nc.alloc_semaphore(name)
        return name

    def add(self, engine, fn, deps=(), inc=None, dma=False, small=False):
        eid = len(self.events)
        deps = tuple(d for d in deps if d is not None)
        for d in deps:
            assert 0 <= d < eid, (d, eid)
        if inc is not None:
            self.sem(inc[0])
            self.last_inc_by_engine[engine] = eid
        else:
            assert not dma
        self.events.append((engine, fn, deps, inc, dma or small))
        self.phases.append(self.phase)
        return eid

    def join(self):
        return list(self.last_inc_by_engine.values())

    def emit_all(self, block):
        cum = {}
        after = [None] * len(self.events)
        for i, (eng, fn, deps, inc, dma) in enumerate(self.events):
            if inc is not None:
                s, n = inc
                cum[s] = cum.get(s, 0) + n
                after[i] = (s, cum[s])
        self.final_counts = dict(cum)

        def emitter(E):
            def run(engobj):
                waited = {}
                cur_scope = None
                for i, (eng, fn, deps, inc, dma) in enumerate(self.events):
                    if eng != E:
                        continue
                    ph = self.phases[i]
                    if ph != cur_scope:
                        if cur_scope is not None:
                            self.nc.pop_named_scope(cur_scope)
                        self.nc.push_named_scope(ph)
                        cur_scope = ph
                    for d in deps:
                        deng, _, _, dinc, ddma = self.events[d]
                        if deng == E and not ddma:
                            continue
                        assert after[d] is not None, f"dep {d} of {i} has no inc"
                        s, v = after[d]
                        if waited.get(s, 0) < v:
                            engobj.wait_ge(self.sems[s], v)
                            waited[s] = v
                    inst = fn(engobj) if fn is not None else None
                    if inc is not None:
                        assert inst is not None
                        s, n = inc
                        inst.then_inc(self.sems[s], n)
                if cur_scope is not None:
                    self.nc.pop_named_scope(cur_scope)
            return run

        block.sync(emitter("sync"))
        block.gpsimd(emitter("gpsimd"))
        block.tensor(emitter("tensor"))
        block.vector(emitter("vector"))
        block.scalar(emitter("scalar"))


def build_nc(upto="F2", dump_scr=()):
    nc = bass.Bass("TRN2", target_bir_lowering=False)
    active = set(PHASES[:PHASES.index(upto) + 1])

    x_own = nc.dram_tensor("x_own", [NTOK, D], F32, kind="ExternalInput").ap()
    x_seq = nc.dram_tensor("x_seq", [T, D], F32, kind="ExternalInput").ap()
    wqT = nc.dram_tensor("wqT", [D, D], F32, kind="ExternalInput").ap()
    wkT = nc.dram_tensor("wkT", [D, D], F32, kind="ExternalInput").ap()
    wvT = nc.dram_tensor("wvT", [D, D], F32, kind="ExternalInput").ap()
    woT = nc.dram_tensor("woT", [D, D], F32, kind="ExternalInput").ap()
    wg_t = nc.dram_tensor("wg_t", [4, 16, 16, 128, 128], F32, kind="ExternalInput").ap()
    wu_t = nc.dram_tensor("wu_t", [4, 16, 16, 128, 128], F32, kind="ExternalInput").ap()
    wd_t = nc.dram_tensor("wd_t", [16, 64, 128, 128], F32, kind="ExternalInput").ap()
    masks = nc.dram_tensor("masks", [4, 4, 128, 1024], BF16, kind="ExternalInput").ap()
    out = nc.dram_tensor("out", [NTOK, D], F32, kind="ExternalOutput").ap()

    def scr_tensor(name, shape, dtype):
        kind = "ExternalOutput" if name in dump_scr else "Internal"
        return nc.dram_tensor(name, shape, dtype, kind=kind).ap()

    q_scr = scr_tensor("q_scr", [D, NTOK], BF16)
    k_scr = scr_tensor("k_scr", [D, T], BF16)
    z_scr = scr_tensor("z_scr", [D, NTOK], BF16)
    x1_scr = scr_tensor("x1_scr", [NTOK, D], F32)
    xln2_scr = scr_tensor("xln2_scr", [NTOK, D], BF16)
    hs_scr = scr_tensor("hs_scr", [4, 128, 64, 512], BF16)
    wd_bf_scr = scr_tensor("wd_bf_scr", [16, 64, 128, 128], BF16)

    identb_t = nc.inline_tensor(np.eye(128, dtype=ml_dtypes.bfloat16), "identb")
    identf_t = nc.inline_tensor(np.eye(128, dtype=np.float32), "identf")
    ones_t = nc.inline_tensor(np.ones((128, 1), dtype=ml_dtypes.bfloat16), "onescol")
    eps_t = nc.inline_tensor(np.full((128, 4), EPS, dtype=np.float32), "epscol")

    P = Plan(nc)
    for s in ("IN", "W", "SC", "SR", "PE", "ACT", "VE", "OUT", "PL"):
        P.sem(s)

    with ExitStack() as root:
        identb = root.enter_context(nc.sbuf_tensor("identb_sb", [128, 128], BF16))
        identf = root.enter_context(nc.sbuf_tensor("identf_sb", [128, 128], F32))
        onescol = root.enter_context(nc.sbuf_tensor("ones_sb", [128, 1], BF16))
        eps_sb = root.enter_context(nc.sbuf_tensor("eps_sb", [128, 4], F32))
        stats = root.enter_context(nc.sbuf_tensor("stats", [128, 64], F32))
        junk = root.enter_context(nc.sbuf_tensor("junk", [128, 2048], BF16))

        def stc(idx):
            return stats[:, idx:idx + 1]

        P.add("gpsimd", lambda g: g.dma_start(identb[:, :], identb_t.ap()),
              inc=("IN", 16), dma=True)
        P.add("gpsimd", lambda g: g.dma_start(identf[:, :], identf_t.ap()),
              inc=("IN", 16), dma=True)
        P.add("gpsimd", lambda g: g.dma_start(onescol[:, :], ones_t.ap()),
              inc=("IN", 16), dma=True)
        e_const = P.add("gpsimd", lambda g: g.dma_start(eps_sb[:, :], eps_t.ap()),
                        inc=("IN", 16), dma=True)
        gate0 = [e_const]

        def psum_set(names_shapes):
            hs = []
            with ExitStack() as st:
                for nm, shape, dt in names_shapes:
                    hs.append(st.enter_context(nc.psum_tensor(nm, shape, dt)))
            return hs

        def ln_stats_events(src_regions, par, extra_deps=()):
            o = 16 * (par % 4)
            s1, s2 = stc(o + 0), stc(o + 1)
            m, msq = stc(o + 2), stc(o + 3)
            var, lnv, rstd = stc(o + 4), stc(o + 5), stc(o + 6)
            s1c = stats[:, o + 8:o + 12]
            s2c = stats[:, o + 12:o + 16]
            nr = len(src_regions)
            ve = []
            ac = []
            if nr == 1:
                ap, dep = src_regions[0]
                ve.append(P.add("vector",
                                lambda v, ap=ap: v.reduce_sum(s1, ap, AX.X),
                                deps=[dep] + list(extra_deps), inc=("VE", 1), small=True))
                ac.append(P.add("scalar",
                                lambda s, ap=ap: s.activation(
                                    junk[:, 0:ap.shape[-1]], ap, AF.Square, accum_out=s2),
                                deps=[dep] + list(extra_deps), inc=("ACT", 1), small=True))
                e_s2ready = ac[-1]
                s2src = s2
            else:
                for r, (ap, dep) in enumerate(src_regions):
                    ve.append(P.add("vector",
                                    lambda v, ap=ap, r=r: v.reduce_sum(
                                        s1c[:, r:r + 1], ap, AX.X),
                                    deps=[dep] + list(extra_deps), inc=("VE", 1), small=True))
                    ac.append(P.add("scalar",
                                    lambda s, ap=ap, r=r: s.activation(
                                        junk[:, 0:ap.shape[-1]], ap, AF.Square,
                                        accum_out=s2c[:, r:r + 1]),
                                    deps=[dep, ve[-1]] + list(extra_deps),
                                    inc=("ACT", 1), small=True))
                ve.append(P.add("vector", lambda v: v.reduce_sum(s1, s1c, AX.X),
                                deps=[ve[-1]], inc=("VE", 1), small=True))
                e_s2ready = P.add("vector", lambda v: v.reduce_sum(s2, s2c, AX.X),
                                  deps=[ac[-1]], inc=("VE", 1), small=True)
                ve.append(e_s2ready)
                s2src = s2
            e_m = P.add("vector", lambda v: v.tensor_single_scalar(m, s1, INV_D, ALU.mult),
                        deps=[ve[-1]], inc=("VE", 1), small=True)
            e_msq = P.add("vector", lambda v: v.tensor_mul(msq, m, m),
                          deps=[e_m], inc=("VE", 1), small=True)
            e_var = P.add("vector",
                          lambda v: v.tensor_scalar(var, s2src, INV_D, msq,
                                                    ALU.mult, ALU.subtract),
                          deps=[e_msq, e_s2ready], inc=("VE", 1), small=True)
            e_ln = P.add("scalar", lambda s: s.activation(lnv, var, AF.Ln,
                                                          bias=eps_sb[:, 0:1]),
                         deps=[e_var], inc=("ACT", 1), small=True)
            e_rs = P.add("scalar", lambda s: s.activation(rstd, lnv, AF.Exp, scale=-0.5),
                         deps=[e_ln], inc=("ACT", 1), small=True)
            return m, rstd, e_rs

        def phase_ln_transpose(src, ntiles, dst, dst_big, xg, lng, pb, gate, gsz):
            nbuf = len(xg)
            ngroups = ntiles // gsz
            e_norm = {}
            e_ev = {}
            e_grp_done = {}
            for gi in range(ngroups):
                par = gi % nbuf
                b = 32 * par
                s1g, s2g = stats[:, b:b + gsz], stats[:, b + 4:b + 4 + gsz]
                mg, msqg = stats[:, b + 8:b + 8 + gsz], stats[:, b + 12:b + 12 + gsz]
                varg = stats[:, b + 16:b + 16 + gsz]
                lnvg = stats[:, b + 20:b + 20 + gsz]
                rstdg = stats[:, b + 24:b + 24 + gsz]
                deps = list(gate)
                if gi >= nbuf:
                    deps.append(e_grp_done[gi - nbuf])
                e_load = P.add("sync",
                               lambda sy, gi=gi, par=par: sy.dma_start(
                                   xg[par][:, :].rearrange("p (n d) -> p n d", d=D),
                                   src[gi * gsz * 128:(gi + 1) * gsz * 128, :]
                                   .rearrange("(n p) d -> p n d", p=128)),
                               deps=deps, inc=("IN", 16), dma=True)
                e_red = P.add("vector",
                              lambda v, par=par, s1g=s1g: v.reduce_sum(
                                  s1g.rearrange("p (n x) -> p n x", x=1),
                                  xg[par][:, :].rearrange("p (n d) -> p n d", d=D),
                                  AX.X),
                              deps=[e_load], inc=("VE", 1), small=True)
                e_sq = None
                for r in range(gsz):
                    e_sq = P.add("scalar",
                                 lambda sc, par=par, r=r, s2g=s2g: sc.activation(
                                     junk[:, 0:D], xg[par][:, r * D:(r + 1) * D],
                                     AF.Square, accum_out=s2g[:, r:r + 1]),
                                 deps=[e_load], inc=("ACT", 1), small=True)
                e_m = P.add("vector",
                            lambda v, s1g=s1g, mg=mg: v.tensor_single_scalar(
                                mg, s1g, INV_D, ALU.mult),
                            deps=[e_red], inc=("VE", 1), small=True)
                e_msq = P.add("vector",
                              lambda v, mg=mg, msqg=msqg: v.tensor_mul(msqg, mg, mg),
                              deps=[e_m], inc=("VE", 1), small=True)
                e_v0 = P.add("vector",
                             lambda v, s2g=s2g, varg=varg: v.tensor_single_scalar(
                                 varg, s2g, INV_D, ALU.mult),
                             deps=[e_sq], inc=("VE", 1), small=True)
                e_v1 = P.add("vector",
                             lambda v, msqg=msqg, varg=varg: v.tensor_sub(
                                 varg, varg, msqg),
                             deps=[e_msq, e_v0], inc=("VE", 1), small=True)
                e_var = P.add("vector",
                              lambda v, varg=varg: v.tensor_single_scalar(
                                  varg, varg, float(EPS), ALU.add),
                              deps=[e_v1], inc=("VE", 1), small=True)
                e_ln = P.add("scalar",
                             lambda sc, varg=varg, lnvg=lnvg: sc.activation(
                                 lnvg, varg, AF.Ln),
                             deps=[e_var], inc=("ACT", 1), small=True)
                e_rs = P.add("scalar",
                             lambda sc, lnvg=lnvg, rstdg=rstdg: sc.activation(
                                 rstdg, lnvg, AF.Exp, scale=-0.5),
                             deps=[e_ln], inc=("ACT", 1), small=True)
                for r in range(gsz):
                    e_norm[(gi, r)] = P.add(
                        "vector",
                        lambda v, par=par, r=r, mg=mg, rstdg=rstdg: v.tensor_scalar(
                            lng[par][:, r * D:(r + 1) * D],
                            xg[par][:, r * D:(r + 1) * D],
                            mg[:, r:r + 1], rstdg[:, r:r + 1],
                            ALU.subtract, ALU.mult),
                        deps=[e_rs], inc=("VE", 1), small=True)
                last = None
                dst3 = dst_big[:, :].rearrange("p (dc t) -> p dc t", t=ntiles * 128)
                for r in range(gsz):
                    i = gi * gsz + r
                    for rr in range(4):
                        g4 = i * 4 + rr
                        bank = g4 % 2
                        e_tp = None
                        for k in range(4):
                            dc = rr * 4 + k
                            deps_tp = [e_norm[(gi, r)]]
                            if g4 >= 2:
                                deps_tp.append(e_ev[g4 - 2])
                            e_tp = P.add("tensor",
                                         lambda t, par=par, r=r, dc=dc, bank=bank,
                                         k=k: t.transpose(
                                             pb[bank][:, k * 128:(k + 1) * 128],
                                             lng[par][:, r * D + dc * 128:
                                                      r * D + (dc + 1) * 128],
                                             identb[:, :]),
                                         deps=(deps_tp if k == 0 else
                                               [e_ev.get(g4 - 2)]),
                                         inc=("PE", 1))
                        e_ev[g4] = P.add("vector",
                                         lambda v, i=i, rr=rr, bank=bank,
                                         dst3=dst3: v.tensor_copy(
                                             dst3[:, rr * 4:(rr + 1) * 4,
                                                  i * 128:(i + 1) * 128],
                                             pb[bank][:, 0:512].rearrange(
                                                 "p (n t) -> p n t", t=128)),
                                         deps=[e_tp], inc=("VE", 1))
                        last = e_ev[g4]
                e_grp_done[gi] = last
            return e_ev

        def phase_proj(wsrc, wbuf, src_T, nc_tiles, stage, dst_scr, ps, gate,
                       wgate=None):
            e_w = [None] * NDC
            for dc in range(NDC):
                e_w[dc] = P.add("gpsimd",
                                lambda g, dc=dc: g.dma_start(
                                    wbuf[dc][:, :], wsrc[dc * 128:(dc + 1) * 128, :]),
                                deps=((wgate if wgate is not None else gate)
                                      if dc == 0 else ()),
                                inc=("W", 16), dma=True)
            e_evac = {}
            e_dma = {}
            nstage = len(stage)
            hw = stage[0].shape[1] // 512
            nh = nc_tiles // hw
            for fc in range(NDC):
                for ncb in range(nc_tiles):
                    k = fc * nc_tiles + ncb
                    par = k % 2
                    kbuf = fc * nh + ncb // hw
                    deps0 = [e_w[NDC - 1]] + list(gate)
                    if k >= 2:
                        deps0.append(e_evac[k - 2])
                    for dc in range(NDC):
                        dd = list(deps0) if dc == 0 else []
                        e = P.add("tensor",
                                  lambda t, fc=fc, ncb=ncb, dc=dc, par=par: t.matmul(
                                      ps[par][:, :],
                                      wbuf[dc][:, fc * 128:(fc + 1) * 128],
                                      src_T[dc][:, ncb * 512:(ncb + 1) * 512],
                                      start=(dc == 0), stop=(dc == NDC - 1)),
                                  deps=dd,
                                  inc=(("PE", 1) if dc == NDC - 1 else None))
                    sdeps = [e, e_dma.get(kbuf - nstage)]
                    e_evac[k] = P.add("vector",
                                      lambda v, ncb=ncb, par=par, kbuf=kbuf,
                                      hw=hw: v.tensor_copy(
                                          stage[kbuf % nstage][:, (ncb % hw) * 512:
                                                               (ncb % hw + 1) * 512],
                                          ps[par][:, :]),
                                      deps=sdeps, inc=("VE", 1))
                    if ncb % hw == hw - 1:
                        e_dma[kbuf] = P.add(
                            "sync",
                            lambda sy, fc=fc, ncb=ncb, kbuf=kbuf, hw=hw: sy.dma_start(
                                dst_scr[fc * 128:(fc + 1) * 128,
                                        (ncb - hw + 1) * 512:(ncb + 1) * 512],
                                stage[kbuf % nstage][:, :]),
                            deps=[e_evac[fc * nc_tiles + ncb]],
                            inc=("SC", 16), dma=True)
            return e_dma

        join_prev = gate0
        P.phase = "A1"
        if "A1" in active:
            with ExitStack() as sc1:
                xolT_big = sc1.enter_context(
                    nc.sbuf_tensor("xolT", [128, NDC * NTOK], BF16))
                xolT = [xolT_big[:, i * NTOK:(i + 1) * NTOK] for i in range(NDC)]
                wq_bf = [sc1.enter_context(nc.sbuf_tensor(f"wq{i}", [128, D], BF16))
                         for i in range(NDC)]
                xbuf = [sc1.enter_context(nc.sbuf_tensor(f"xa{i}", [128, 2 * D], F32))
                        for i in range(2)]
                lnbuf = [sc1.enter_context(nc.sbuf_tensor(f"lna{i}", [128, 2 * D], BF16))
                         for i in range(2)]
                qstage = [sc1.enter_context(nc.sbuf_tensor(f"qst{i}", [128, NTOK // 2], BF16))
                          for i in range(2)]
                psA = psum_set([("pbA0", [128, 1024], BF16), ("pbA1", [128, 1024], BF16),
                                ("psq0", [128, 512], F32), ("psq1", [128, 512], F32)])
                pb, psq = psA[0:2], psA[2:4]

                phase_ln_transpose(x_own, NTOK // 128, xolT, xolT_big, xbuf, lnbuf, pb, join_prev, 2)
                join_prev = P.join()
                if "xol_dump" in dump_scr:
                    xd = nc.dram_tensor("xol_dump", [D, NTOK], BF16,
                                        kind="ExternalOutput").ap()
                    for dc in range(NDC):
                        P.add("sync",
                              lambda sy, dc=dc, xd=xd: sy.dma_start(
                                  xd[dc * 128:(dc + 1) * 128, :], xolT[dc][:, :]),
                              deps=join_prev, inc=("OUT", 16), dma=True)
                if "BQ" in active:
                    P.phase = "BQ"
                    phase_proj(wqT, wq_bf, xolT, NTOK // 512, qstage, q_scr, psq,
                               join_prev, wgate=gate0)
                    join_prev = P.join()

        if "A2" in active:
            P.phase = "A2"
            sc2 = ExitStack()
            xln1T_big = sc2.enter_context(
                nc.sbuf_tensor("x1T", [128, NDC * T], BF16))
            xln1T = [xln1T_big[:, i * T:(i + 1) * T] for i in range(NDC)]
            with ExitStack() as sc2a:
                xbuf = [sc2a.enter_context(nc.sbuf_tensor(f"xb{i}", [128, 2 * D], F32))
                        for i in range(2)]
                lnbuf = [sc2a.enter_context(nc.sbuf_tensor(f"lnb{i}", [128, 2 * D], BF16))
                         for i in range(2)]
                pb = psum_set([("pbB0", [128, 1024], BF16), ("pbB1", [128, 1024], BF16)])
                phase_ln_transpose(x_seq, T // 128, xln1T, xln1T_big, xbuf, lnbuf, pb, join_prev, 2)
                join_prev = P.join()

            if "BK" in active:
                P.phase = "BK"
                with ExitStack() as sc2b:
                    wk_bf = [sc2b.enter_context(nc.sbuf_tensor(f"wk{i}", [128, D], BF16))
                             for i in range(NDC)]
                    kstage = [sc2b.enter_context(nc.sbuf_tensor(f"kst{i}", [128, T // 2], BF16))
                              for i in range(2)]
                    psk = psum_set([("psk0", [128, 512], F32), ("psk1", [128, 512], F32)])
                    phase_proj(wkT, wk_bf, xln1T, T // 512, kstage, k_scr, psk, join_prev)
                    join_prev = P.join()

            if "BV" in active:
                P.phase = "BV"
                with ExitStack() as sc2c:
                    wv_bf = [sc2c.enter_context(nc.sbuf_tensor(f"wv{i}", [128, D], BF16))
                             for i in range(NDC)]
                    psv = psum_set([(f"psv{i}", [128, 512], F32) for i in range(8)])
                    e_w = [None] * NDC
                    for dc in range(NDC):
                        e_w[dc] = P.add("gpsimd",
                                        lambda g, dc=dc: g.dma_start(
                                            wv_bf[dc][:, :],
                                            wvT[dc * 128:(dc + 1) * 128, :]),
                                        deps=(join_prev if dc == 0 else ()),
                                        inc=("W", 16), dma=True)
                    e_evl = {}
                    for ct in range(T // 128):
                        par = ct % 2
                        e_stop = None
                        for nf in range(4):
                            deps0 = [e_w[NDC - 1]] + list(join_prev)
                            if ct >= 2:
                                deps0 += list(e_evl[ct - 2])
                            for dc in range(NDC):
                                dd = list(deps0) if dc == 0 else []
                                e_stop = P.add(
                                    "tensor",
                                    lambda t, ct=ct, nf=nf, dc=dc, par=par: t.matmul(
                                        psv[4 * par + nf][:, :],
                                        xln1T[dc][:, ct * 128:(ct + 1) * 128],
                                        wv_bf[dc][:, nf * 512:(nf + 1) * 512],
                                        start=(dc == 0), stop=(dc == NDC - 1)),
                                    deps=dd,
                                    inc=(("PE", 1) if dc == NDC - 1 else None))
                        last_ve = last_ac = None
                        for dc in range(NDC):
                            if dc < 8:
                                last_ve = P.add(
                                    "vector",
                                    lambda v, ct=ct, dc=dc, par=par: v.tensor_copy(
                                        xln1T[dc][:, ct * 128:(ct + 1) * 128],
                                        psv[4 * par + dc // 4][:, (dc % 4) * 128:
                                                               (dc % 4 + 1) * 128]),
                                    deps=[e_stop], inc=("VE", 1))
                            else:
                                last_ac = P.add(
                                    "scalar",
                                    lambda s, ct=ct, dc=dc, par=par: s.copy(
                                        xln1T[dc][:, ct * 128:(ct + 1) * 128],
                                        psv[4 * par + dc // 4][:, (dc % 4) * 128:
                                                               (dc % 4 + 1) * 128]),
                                    deps=[e_stop], inc=("ACT", 1))
                        e_evl[ct] = (last_ve, last_ac)
                    join_prev = P.join()
                    if "v_dump" in dump_scr:
                        vd = nc.dram_tensor("v_dump", [D, T], BF16,
                                            kind="ExternalOutput").ap()
                        for dc in range(NDC):
                            P.add("sync",
                                  lambda sy, dc=dc, vd=vd: sy.dma_start(
                                      vd[dc * 128:(dc + 1) * 128, :], xln1T[dc][:, :]),
                                  deps=join_prev, inc=("OUT", 16), dma=True)

            if "C" in active:
                P.phase = "C"
                with ExitStack() as scC:
                    khb = [scC.enter_context(nc.sbuf_tensor(f"kh{i}", [128, T], BF16))
                           for i in range(2)]
                    qhb = [scC.enter_context(nc.sbuf_tensor(f"qh{i}", [128, 512], BF16))
                           for i in range(2)]
                    mkb = [scC.enter_context(nc.sbuf_tensor(f"mk{i}", [128, 4096], BF16))
                           for i in range(2)]
                    pbuf = [scC.enter_context(nc.sbuf_tensor(f"pp{i}", [128, 1024], BF16))
                            for i in range(8)]
                    zsb = [scC.enter_context(nc.sbuf_tensor(f"zs{i}", [128, 512], BF16))
                           for i in range(2)]
                    zstage = [scC.enter_context(nc.sbuf_tensor(f"zt{i}", [128, 512], BF16))
                              for i in range(2)]
                    rsb = [scC.enter_context(nc.sbuf_tensor(f"rs{i}", [128, 4], F32))
                           for i in range(2)]
                    psC = psum_set([("psS0", [128, 1024], F32), ("psS1", [128, 1024], F32),
                                    ("psZ0", [128, 512], F32), ("psZ1", [128, 512], F32),
                                    ("psSum", [128, 4], F32),
                                    ("psZtp", [128, 1024], BF16)])
                    psS = psC[0:2]
                    psZ = psC[2:4]
                    psSum = psC[4]
                    psZtp = psC[5]

                    psS_last = {0: None, 1: None}
                    pbuf_last = {i: None for i in range(8)}
                    e_mk = {}
                    e_mk_lastuse = {}
                    e_scale = {}
                    e_recip = {}
                    e_zev = {}
                    e_zdma = {}
                    e_avl = {}
                    hseq = [(j, h) for j in range(4) for h in range(NH)]

                    def emit_tail(j, h, hi, par, e2):
                        e_recip[(j, h)] = P.add("vector",
                                                lambda v, par=par: v.reciprocal(
                                                    rsb[par][:, :], psSum[:, :]),
                                                deps=[e2], inc=("VE", 1), small=True)
                        e_sc = None
                        for qs in range(4):
                            e_sc = P.add("vector",
                                         lambda v, par=par, qs=qs: v.tensor_scalar(
                                             zsb[par][:, qs * 128:(qs + 1) * 128],
                                             psZ[par][:, qs * 128:(qs + 1) * 128],
                                             rsb[par][:, qs:qs + 1], None, ALU.mult),
                                         deps=[e_recip[(j, h)]], inc=("VE", 1))
                        e_scale[(j, h)] = e_sc
                        e_tp = None
                        for qs in range(4):
                            deps_t = [e_sc]
                            if hi >= 1:
                                deps_t.append(e_zev[hseq[hi - 1]])
                            e_tp = P.add("tensor",
                                         lambda t, par=par, qs=qs: t.transpose(
                                             psZtp[:, qs * 128:(qs + 1) * 128],
                                             zsb[par][:, qs * 128:(qs + 1) * 128],
                                             identb[:, :]),
                                         deps=(deps_t if qs == 0 else ()), inc=("PE", 1))
                        deps_z = [e_tp]
                        if hi >= 2:
                            deps_z.append(e_zdma[hseq[hi - 2]])
                        e_zev[(j, h)] = P.add("vector",
                                              lambda v, par=par: v.tensor_copy(
                                                  zstage[par][:, :], psZtp[:, 0:512]),
                                              deps=deps_z, inc=("VE", 1))
                        e_zdma[(j, h)] = P.add("sync",
                                               lambda sy, j=j, h=h, par=par: sy.dma_start(
                                                   z_scr[h * 128:(h + 1) * 128,
                                                         j * 512:(j + 1) * 512],
                                                   zstage[par][:, :]),
                                               deps=[e_zev[(j, h)]], inc=("SC", 16),
                                               dma=True)

                    def emit_avsums(j, h, hi, par, grp, spar, dep_av, nsub):
                        e2 = None
                        for s2 in range(2):
                            c = 2 * grp + s2
                            for qs in range(4):
                                first = (c == 0 and qs == 0)
                                lastt = (c == nsub - 1 and qs == 3)
                                deps_av = []
                                if s2 == 0 and qs == 0:
                                    deps_av.append(dep_av)
                                if first:
                                    if hi >= 2:
                                        deps_av.append(e_scale[hseq[hi - 2]])
                                    if hi >= 1:
                                        deps_av.append(e_recip[hseq[hi - 1]])
                                lo = s2 * 512 + qs * 128
                                P.add("tensor",
                                      lambda t, h=h, c=c, par=par, spar=spar,
                                      qs=qs, lo=lo, first=first, lastt=lastt: t.matmul(
                                          psZ[par][:, qs * 128:(qs + 1) * 128],
                                          pbuf[spar][:, lo:lo + 128],
                                          xln1T[h][:, c * 128:(c + 1) * 128],
                                          start=first, stop=lastt),
                                      deps=deps_av, inc=("PE", 1))
                                e2 = P.add("tensor",
                                           lambda t, spar=spar, qs=qs, lo=lo,
                                           first=first, lastt=lastt: t.matmul(
                                               psSum[:, qs:qs + 1],
                                               pbuf[spar][:, lo:lo + 128],
                                               onescol[:, :],
                                               start=first, stop=lastt),
                                           deps=(), inc=("PE", 1))
                        pbuf_last[spar] = e2
                        return e2

                    pending_tail = None
                    for hi, (j, h) in enumerate(hseq):
                        par = hi % 2
                        nsub = 4 * CAP[j]
                        ngrp = nsub // 2
                        deps_k = list(join_prev)
                        if hi >= 2:
                            deps_k.append(e_avl[hseq[hi - 2]])
                        e_k = P.add("sync",
                                    lambda sy, h=h, par=par, nsub=nsub: sy.dma_start(
                                        khb[par][:, 0:nsub * 128],
                                        k_scr[h * 128:(h + 1) * 128, 0:nsub * 128]),
                                    deps=deps_k, inc=("SR", 16), dma=True)
                        e_q = P.add("sync",
                                    lambda sy, j=j, h=h, par=par: sy.dma_start(
                                        qhb[par][:, :],
                                        q_scr[h * 128:(h + 1) * 128, j * 512:(j + 1) * 512]),
                                    deps=deps_k, inc=("SR", 16), dma=True)
                        if h == 0:
                            deps_m = list(join_prev)
                            if j >= 2:
                                deps_m.append(e_mk_lastuse[j - 2])
                            e_mk[j] = P.add(
                                "sync",
                                lambda sy, j=j: sy.dma_start(
                                    mkb[j % 2][:, :].rearrange("p (g q) -> p g q", q=1024),
                                    masks[j].rearrange("g p q -> p g q")),
                                deps=deps_m, inc=("SR", 16), dma=True)
                        pend = None
                        for grp in range(ngrp):
                            spar = grp % 2
                            ppar = (hi * 20 + grp) % 8
                            masked = grp >= ngrp - 4
                            gi = grp - (ngrp - 4)
                            deps_s = [e_k, e_q, psS_last[spar]]
                            for s2 in range(2):
                                c = 2 * grp + s2
                                e_sg = P.add("tensor",
                                             lambda t, c=c, s2=s2, spar=spar, par=par: t.matmul(
                                                 psS[spar][:, s2 * 512:(s2 + 1) * 512],
                                                 khb[par][:, c * 128:(c + 1) * 128],
                                                 qhb[par][:, :],
                                                 start=True, stop=True),
                                             deps=(deps_s if s2 == 0 else ()),
                                             inc=("PE", 1))
                            e_exp = P.add("scalar",
                                          lambda s, spar=spar, ppar=ppar: s.activation(
                                              pbuf[ppar][:, :], psS[spar][:, :],
                                              AF.Exp, scale=SCALE),
                                          deps=[e_sg, pbuf_last[ppar]], inc=("ACT", 1))
                            psS_last[spar] = e_exp
                            dep_av = e_exp
                            if masked:
                                dep_av = P.add("vector",
                                               lambda v, j=j, gi=gi, ppar=ppar: v.tensor_mul(
                                                   pbuf[ppar][:, :], pbuf[ppar][:, :],
                                                   mkb[j % 2][:, gi * 1024:(gi + 1) * 1024]),
                                               deps=[e_exp, e_mk[j]], inc=("VE", 1))
                                e_mk_lastuse[j] = dep_av
                            if grp == 0 and pending_tail is not None:
                                pending_tail()
                                pending_tail = None
                            if pend is not None:
                                emit_avsums(*pend)
                            pend = (j, h, hi, par, grp, ppar, dep_av, nsub)
                        e2 = emit_avsums(*pend)
                        e_avl[(j, h)] = e2
                        pending_tail = (lambda j=j, h=h, hi=hi, par=par, e2=e2:
                                        emit_tail(j, h, hi, par, e2))
                    pending_tail()
                    join_prev = P.join()
            sc2.close()

        if "D" in active:
            P.phase = "D"
            with ExitStack() as scD:
                zT = [scD.enter_context(nc.sbuf_tensor(f"zT{i}", [128, NTOK], BF16))
                      for i in range(NDC)]
                wo_bf = [scD.enter_context(nc.sbuf_tensor(f"wo{i}", [128, D], BF16))
                         for i in range(NDC)]
                xob = [scD.enter_context(nc.sbuf_tensor(f"xo{i}", [128, D], F32))
                       for i in range(2)]
                t1b = scD.enter_context(nc.sbuf_tensor("t1b", [128, D], F32))
                x1b = [scD.enter_context(nc.sbuf_tensor(f"x1b{i}", [128, D], F32))
                       for i in range(2)]
                xl2b = [scD.enter_context(nc.sbuf_tensor(f"xl2b{i}", [128, D], BF16))
                        for i in range(2)]
                pso = psum_set([("psoA", [128, 2048], F32), ("psoB", [128, 2048], F32)])

                e_z = [None] * NDC
                e_w = [None] * NDC
                for fc in range(NDC):
                    e_z[fc] = P.add("sync",
                                    lambda sy, fc=fc: sy.dma_start(
                                        zT[fc][:, :], z_scr[fc * 128:(fc + 1) * 128, :]),
                                    deps=(join_prev if fc == 0 else ()),
                                    inc=("SR", 16), dma=True)
                    e_w[fc] = P.add("gpsimd",
                                    lambda g, fc=fc: g.dma_start(
                                        wo_bf[fc][:, :], woT[fc * 128:(fc + 1) * 128, :]),
                                    deps=(join_prev if fc == 0 else ()),
                                    inc=("W", 16), dma=True)
                e_res = {}
                e_x1dma = {}
                e_xl2dma = {}
                for m in range(NTOK // 128):
                    par = m % 2
                    e_xo = P.add("sync",
                                 lambda sy, m=m, par=par: sy.dma_start(
                                     xob[par][:, :], x_own[m * 128:(m + 1) * 128, :]),
                                 deps=([e_res[m - 2]] if m >= 2 else list(join_prev)),
                                 inc=("IN", 16), dma=True)
                    e_stop = {}
                    for nf in range(4):
                        deps0 = [e_z[NDC - 1], e_w[NDC - 1]]
                        if m >= 2:
                            deps0.append(e_res[m - 2])
                        for fc in range(NDC):
                            e = P.add("tensor",
                                      lambda t, m=m, nf=nf, fc=fc, par=par: t.matmul(
                                          pso[par][:, nf * 512:(nf + 1) * 512],
                                          zT[fc][:, m * 128:(m + 1) * 128],
                                          wo_bf[fc][:, nf * 512:(nf + 1) * 512],
                                          start=(fc == 0), stop=(fc == NDC - 1)),
                                      deps=(deps0 if fc == 0 else ()),
                                      inc=(("PE", 1) if fc == NDC - 1 else None))
                        e_stop[nf] = e
                    regions = [(pso[par][:, nf * 512:(nf + 1) * 512], e_stop[nf])
                               for nf in range(4)]
                    m_ap, r_ap, e_rs = ln_stats_events(regions, par)
                    e_n = P.add("vector",
                                lambda v, par=par, m_ap=m_ap,
                                r_ap=r_ap: v.tensor_scalar(
                                    t1b[:, :], pso[par][:, :], m_ap, r_ap,
                                    ALU.subtract, ALU.mult),
                                deps=[e_rs], inc=("VE", 1))
                    deps_a = [e_n, e_xo]
                    if m >= 2:
                        deps_a.append(e_x1dma[m - 2])
                    e_add = P.add("vector",
                                  lambda v, par=par: v.tensor_add(
                                      x1b[par][:, :], t1b[:, :], xob[par][:, :]),
                                  deps=deps_a, inc=("VE", 1))
                    e_x1dma[m] = P.add("sync",
                                       lambda sy, m=m, par=par: sy.dma_start(
                                           x1_scr[m * 128:(m + 1) * 128, :],
                                           x1b[par][:, :]),
                                       deps=[e_add], inc=("SC", 16), dma=True)
                    m2_ap, r2_ap, e_rs2 = ln_stats_events([(x1b[par][:, :], e_add)], par)
                    deps_r = [e_rs2]
                    if m >= 2:
                        deps_r.append(e_xl2dma[m - 2])
                    e_res[m] = P.add("vector",
                                     lambda v, par=par, m2=m2_ap, r2=r2_ap: v.tensor_scalar(
                                         xl2b[par][:, :], x1b[par][:, :], m2, r2,
                                         ALU.subtract, ALU.mult),
                                     deps=deps_r, inc=("VE", 1))
                    e_xl2dma[m] = P.add("sync",
                                        lambda sy, m=m, par=par: sy.dma_start(
                                            xln2_scr[m * 128:(m + 1) * 128, :],
                                            xl2b[par][:, :]),
                                        deps=[e_res[m]], inc=("SC", 16), dma=True)
                join_prev = P.join()

        if "F0" in active:
            P.phase = "F0"
            scF = ExitStack()
            xl2T_big = scF.enter_context(
                nc.sbuf_tensor("x2T", [128, NDC * NTOK], BF16))
            xl2T = [xl2T_big[:, i * NTOK:(i + 1) * NTOK] for i in range(NDC)]
            with ExitStack() as scF0:
                xtile = [scF0.enter_context(nc.sbuf_tensor(f"f0x{i}", [128, D], BF16))
                         for i in range(4)]
                pb = psum_set([("pbF0", [128, 1024], BF16), ("pbF1", [128, 1024], BF16)])
                e_ev = {}
                for i in range(NTOK // 128):
                    par = i % 4
                    deps = list(join_prev)
                    if i >= 4:
                        deps.append(e_ev[(i - 4) * NDC + NDC - 1])
                    e_ld = P.add("sync",
                                 lambda sy, i=i, par=par: sy.dma_start(
                                     xtile[par][:, :],
                                     xln2_scr[i * 128:(i + 1) * 128, :]),
                                 deps=deps, inc=("SR", 16), dma=True)
                    for dc in range(NDC):
                        g = i * NDC + dc
                        bank, sl = g % 2, (g // 2) % 8
                        deps_tp = [e_ld]
                        if g >= 2:
                            deps_tp.append(e_ev[g - 2])
                        e_tp = P.add("tensor",
                                     lambda t, par=par, dc=dc, bank=bank, sl=sl: t.transpose(
                                         pb[bank][:, sl * 128:(sl + 1) * 128],
                                         xtile[par][:, dc * 128:(dc + 1) * 128],
                                         identb[:, :]),
                                     deps=deps_tp, inc=("PE", 1))
                        e_ev[g] = P.add("vector",
                                        lambda v, i=i, dc=dc, bank=bank, sl=sl: v.tensor_copy(
                                            xl2T[dc][:, i * 128:(i + 1) * 128],
                                            pb[bank][:, sl * 128:(sl + 1) * 128]),
                                        deps=[e_tp], inc=("VE", 1))
                join_prev = P.join()

            if "F1" in active:
                P.phase = "F1"
                with ExitStack() as scF1:
                    wgb = [scF1.enter_context(nc.sbuf_tensor(f"wgb{i}", [128, D], BF16))
                           for i in range(8)]
                    wub = [scF1.enter_context(nc.sbuf_tensor(f"wub{i}", [128, D], BF16))
                           for i in range(8)]
                    gbuf = [scF1.enter_context(nc.sbuf_tensor(f"gbuf{i}", [128, 512], BF16))
                            for i in range(2)]
                    hbuf = [scF1.enter_context(nc.sbuf_tensor(f"hb{i}", [128, 2048], BF16))
                            for i in range(4)]
                    abuf = [scF1.enter_context(nc.sbuf_tensor(f"ab{i}", [128, 512], BF16))
                            for i in range(4)]
                    bbuf = [scF1.enter_context(nc.sbuf_tensor(f"bb{i}", [128, 512], BF16))
                            for i in range(4)]
                    cbuf = scF1.enter_context(nc.sbuf_tensor("cb", [128, 512], BF16))
                    hsp = [scF1.enter_context(nc.sbuf_tensor(f"hsp{i}", [128, 2048], BF16))
                           for i in range(4)]
                    wdst = [scF1.enter_context(nc.sbuf_tensor("wdst0", [128, 8192], BF16))]
                    psF = psum_set([("psg0", [128, 512], F32), ("psg1", [128, 512], F32),
                                    ("psu0", [128, 512], F32), ("psu1", [128, 512], F32)])

                    e_hdma = {}
                    e_out_prev = {}
                    e_ge_prev = {}
                    e_hm = {}
                    e_wu_stop = {}
                    e_wdcast = {}
                    e_wdwrite = {}
                    for gc in range(16):
                        gpar = gc % 2
                        e_wdcast[gc] = P.add(
                            "gpsimd",
                            lambda g, gc=gc, gpar=gpar: g.dma_start(
                                wdst[0][:, :].rearrange("a (ic b) -> a ic b", b=128),
                                wd_t[gc].rearrange("ic a b -> a ic b")),
                            deps=([e_wdwrite[gc - 1]] if gc >= 1 else list(join_prev)),
                            inc=("W", 16), dma=True)
                        e_wdwrite[gc] = P.add(
                            "sync",
                            lambda sy, gc=gc, gpar=gpar: sy.dma_start(
                                wd_bf_scr[gc].rearrange("ic a b -> a ic b"),
                                wdst[0][:, :].rearrange("a (ic b) -> a ic b", b=128)),
                            deps=[e_wdcast[gc]], inc=("SC", 16), dma=True)
                        e_wg = {}
                        e_wu = {}
                        for j in range(4):
                            deps_w = list(join_prev)
                            if gc >= 2:
                                deps_w.append(e_wu_stop[(gc - 2, j)])
                            e_wg[j] = P.add("gpsimd",
                                            lambda g, j=j, gc=gc, gpar=gpar: g.dma_start(
                                                wgb[4 * gpar + j][:, :].rearrange(
                                                    "a (dc b) -> a dc b", b=128),
                                                wg_t[j, gc].rearrange("dc a b -> a dc b")),
                                            deps=deps_w, inc=("W", 16), dma=True)
                            e_wu[j] = P.add("gpsimd",
                                            lambda g, j=j, gc=gc, gpar=gpar: g.dma_start(
                                                wub[4 * gpar + j][:, :].rearrange(
                                                    "a (dc b) -> a dc b", b=128),
                                                wu_t[j, gc].rearrange("dc a b -> a dc b")),
                                            deps=deps_w, inc=("W", 16), dma=True)
                        for tN in range(4):
                            for j in range(4):
                                idx = (gc * 4 + tN) * 4 + j
                                ppar = idx % 2
                                deps_g = [e_wg[j], e_hm.get(idx - 2)]
                                for dc in range(NDC):
                                    e_g = P.add("tensor",
                                                lambda t, dc=dc, tN=tN, j=j, gpar=gpar,
                                                ppar=ppar: t.matmul(
                                                    psF[ppar][:, :],
                                                    wgb[4 * gpar + j][:, dc * 128:(dc + 1) * 128],
                                                    xl2T[dc][:, tN * 512:(tN + 1) * 512],
                                                    start=(dc == 0), stop=(dc == NDC - 1)),
                                                deps=(deps_g if dc == 0 else ()),
                                                inc=(("PE", 1) if dc == NDC - 1 else None))
                                e_silu = P.add("scalar",
                                               lambda s, ppar=ppar: s.activation(
                                                   gbuf[ppar][:, :], psF[ppar][:, :],
                                                   AF.Silu),
                                               deps=[e_g, e_hm.get(idx - 2)],
                                               inc=("ACT", 1))
                                deps_u = [e_wu[j], e_hm.get(idx - 2)]
                                for dc in range(NDC):
                                    e_u = P.add("tensor",
                                                lambda t, dc=dc, tN=tN, j=j, gpar=gpar,
                                                ppar=ppar: t.matmul(
                                                    psF[2 + ppar][:, :],
                                                    wub[4 * gpar + j][:, dc * 128:(dc + 1) * 128],
                                                    xl2T[dc][:, tN * 512:(tN + 1) * 512],
                                                    start=(dc == 0), stop=(dc == NDC - 1)),
                                                deps=(deps_u if dc == 0 else ()),
                                                inc=(("PE", 1) if dc == NDC - 1 else None))
                                e_wu_stop[(gc, j)] = e_u
                                deps_hm = [e_u, e_silu, e_out_prev.get((tN, j))]
                                e_hm[idx] = P.add("vector",
                                                  lambda v, j=j, tN=tN, ppar=ppar: v.tensor_mul(
                                                      hbuf[j][:, tN * 512:(tN + 1) * 512],
                                                      psF[2 + ppar][:, :], gbuf[ppar][:, :]),
                                                  deps=deps_hm, inc=("VE", 1))
                            sl = (tN * 512, (tN + 1) * 512)
                            e_abs = []
                            for jj in range(4):
                                e_abs.append(P.add("scalar",
                                                   lambda s, jj=jj, sl=sl: s.activation(
                                                       abuf[jj][:, :],
                                                       hbuf[jj][:, sl[0]:sl[1]], AF.Abs),
                                                   deps=[e_hm[(gc * 4 + tN) * 4 + jj],
                                                         e_ge_prev.get(jj)],
                                                   inc=("ACT", 1)))
                            em1 = P.add("vector",
                                        lambda v: v.tensor_max(bbuf[0][:, :], abuf[0][:, :],
                                                               abuf[1][:, :]),
                                        deps=[e_abs[0], e_abs[1]], inc=("VE", 1))
                            en1 = P.add("vector",
                                        lambda v: v.tensor_tensor(bbuf[1][:, :],
                                                                  abuf[0][:, :],
                                                                  abuf[1][:, :], ALU.min),
                                        deps=[e_abs[1]], inc=("VE", 1))
                            em2 = P.add("vector",
                                        lambda v: v.tensor_max(bbuf[2][:, :], abuf[2][:, :],
                                                               abuf[3][:, :]),
                                        deps=[e_abs[2], e_abs[3]], inc=("VE", 1))
                            en2 = P.add("vector",
                                        lambda v: v.tensor_tensor(bbuf[3][:, :],
                                                                  abuf[2][:, :],
                                                                  abuf[3][:, :], ALU.min),
                                        deps=[e_abs[3]], inc=("VE", 1))
                            es = P.add("vector",
                                       lambda v: v.tensor_tensor(cbuf[:, :], bbuf[0][:, :],
                                                                 bbuf[2][:, :], ALU.min),
                                       deps=[em2], inc=("VE", 1))
                            et = P.add("vector",
                                       lambda v: v.tensor_max(bbuf[1][:, :], bbuf[1][:, :],
                                                              bbuf[3][:, :]),
                                       deps=[en2], inc=("VE", 1))
                            ethr = P.add("vector",
                                         lambda v: v.tensor_max(cbuf[:, :], cbuf[:, :],
                                                                bbuf[1][:, :]),
                                         deps=[es, et], inc=("VE", 1))
                            for jj in range(4):
                                ege = P.add("vector",
                                            lambda v, jj=jj: v.tensor_tensor(
                                                bbuf[jj][:, :], abuf[jj][:, :],
                                                cbuf[:, :], ALU.is_ge),
                                            deps=[ethr], inc=("VE", 1))
                                e_ge_prev[jj] = ege
                                e_out_prev[(tN, jj)] = P.add(
                                    "vector",
                                    lambda v, jj=jj, sl=sl: v.tensor_mul(
                                        hsp[jj][:, sl[0]:sl[1]],
                                        hbuf[jj][:, sl[0]:sl[1]], bbuf[jj][:, :]),
                                    deps=[ege, e_hdma.get((gc - 1, jj))],
                                    inc=("VE", 1))
                        for j in range(4):
                            deps_h = [e_out_prev[(tN, j)] for tN in range(4)]
                            e_hdma[(gc, j)] = P.add(
                                "sync",
                                lambda sy, gc=gc, j=j: sy.dma_start(
                                    hs_scr[:, :, j * 16 + gc,
                                           :].rearrange("tq p t -> p tq t"),
                                    hsp[j][:, :].rearrange("p (tq t) -> p tq t", t=512)),
                                deps=deps_h, inc=("SC", 16), dma=True)
                    join_prev = P.join()
            scF.close()

        e_outdma = {}
        P.phase = "F2"
        if "o2_dump" in dump_scr:
            o2_dump = nc.dram_tensor("o2_dump", [NTOK, D], F32,
                                     kind="ExternalOutput").ap()
        if "F2" in active:
            with ExitStack() as scG:
                hq = scG.enter_context(nc.sbuf_tensor("hq", [128, 64 * 512], BF16))
                wdb = [scG.enter_context(nc.sbuf_tensor(f"wdb{i}", [128, 8192], BF16))
                       for i in range(2)]
                dnst = [scG.enter_context(nc.sbuf_tensor(f"dnst{i}", [128, 512], F32))
                        for i in range(2)]
                o2tm = [scG.enter_context(nc.sbuf_tensor(f"o2tm{i}", [128, D], F32))
                        for i in range(4)]
                x1t = [scG.enter_context(nc.sbuf_tensor(f"x1t{i}", [128, D], F32))
                       for i in range(2)]
                outst = [scG.enter_context(nc.sbuf_tensor(f"outst{i}", [128, D], F32))
                         for i in range(2)]
                t2b = scG.enter_context(nc.sbuf_tensor("t2b", [128, D], F32))
                psd = psum_set([("psd0", [128, 512], F32), ("psd1", [128, 512], F32),
                                ("pst0", [128, 512], F32), ("pst1", [128, 512], F32)])

                e_fin = {}
                e_it_stop = {}
                e_tpe = {}
                e_nrm_prev = {}
                for tq in range(4):
                    deps_h = list(join_prev)
                    if tq >= 1:
                        deps_h.append(e_it_stop[tq * NDC - 1])
                    e_ha = P.add("sync",
                                 lambda sy, tq=tq: sy.dma_start(
                                     hq[:, 0:32 * 512],
                                     hs_scr[tq, :, 0:32, :].rearrange(
                                         "p ic t -> p (ic t)")),
                                 deps=deps_h, inc=("SR", 16), dma=True)
                    e_hb = P.add("scalar",
                                 lambda sc, tq=tq: sc.dma_start(
                                     hq[:, 32 * 512:],
                                     hs_scr[tq, :, 32:, :].rearrange(
                                         "p ic t -> p (ic t)")),
                                 deps=deps_h, inc=("SR", 16), dma=True)
                    e_h = e_hb
                    for dc in range(NDC):
                        it = tq * NDC + dc
                        par = it % 2
                        deps_w = list(join_prev)
                        if it >= 2:
                            deps_w.append(e_it_stop[it - 2])
                        e_wd2 = P.add("gpsimd",
                                      lambda g, dc=dc, par=par: g.dma_start(
                                          wdb[par][:, :].rearrange("a (ic b) -> a ic b", b=128),
                                          wd_bf_scr[dc].rearrange("ic a b -> a ic b")),
                                      deps=deps_w, inc=("W", 16), dma=True)
                        deps0 = [e_wd2, e_ha, e_hb]
                        if it >= 2:
                            deps0.append(e_tpe[(it - 2) * 4 + 3])
                        for ic in range(64):
                            e = P.add("tensor",
                                      lambda t, ic=ic, par=par: t.matmul(
                                          psd[par][:, :],
                                          wdb[par][:, ic * 128:(ic + 1) * 128],
                                          hq[:, ic * 512:(ic + 1) * 512],
                                          start=(ic == 0), stop=(ic == 63)),
                                      deps=(deps0 if ic == 0 else ()),
                                      inc=(("PE", 1) if ic == 63 else None))
                        e_it_stop[it] = e
                        e_dn = P.add("vector",
                                     lambda v, par=par: v.tensor_copy(
                                         dnst[par][:, :], psd[par][:, :]),
                                     deps=[e], inc=("VE", 1))
                        for mq in range(4):
                            tt = it * 4 + mq
                            bank = 2 + tt % 2
                            col = ((tt // 2) % 2) * 128
                            deps_tp = [e_dn, e_tpe.get(tt - 2),
                                       e_nrm_prev.get(mq) if dc == 0 else None]
                            e_tp = P.add("tensor",
                                         lambda t, bank=bank, col=col, par=par,
                                         mq=mq: t.transpose(
                                             psd[bank][:, col:col + 128],
                                             dnst[par][:, mq * 128:(mq + 1) * 128],
                                             identf[:, :]),
                                         deps=deps_tp, inc=("PE", 1))
                            e_tpe[tt] = P.add(
                                "vector",
                                lambda v, bank=bank, col=col, mq=mq, dc=dc: v.tensor_copy(
                                    o2tm[mq][:, dc * 128:(dc + 1) * 128],
                                    psd[bank][:, col:col + 128]),
                                deps=[e_tp], inc=("VE", 1))
                    for mq in range(4):
                        par = mq % 2
                        mglob = tq * 4 + mq
                        if "o2_dump" in dump_scr:
                            P.add("sync",
                                  lambda sy, mq=mq, mglob=mglob: sy.dma_start(
                                      o2_dump[mglob * 128:(mglob + 1) * 128, :],
                                      o2tm[mq][:, :]),
                                  deps=[e_tpe[(tq * NDC + NDC - 1) * 4 + mq]],
                                  inc=("OUT", 16), dma=True)
                        e_x1 = P.add("sync",
                                     lambda sy, mglob=mglob, par=par: sy.dma_start(
                                         x1t[par][:, :],
                                         x1_scr[mglob * 128:(mglob + 1) * 128, :]),
                                     deps=([e_fin[mglob - 2]] if mglob >= 2
                                           else list(join_prev)),
                                     inc=("SR", 16), dma=True)
                        dep_o2 = e_tpe[(tq * NDC + NDC - 1) * 4 + mq]
                        m3, r3, e_rs3 = ln_stats_events([(o2tm[mq][:, :], dep_o2)], par)
                        e_n = P.add("vector",
                                    lambda v, mq=mq, m3=m3, r3=r3: v.tensor_scalar(
                                        t2b[:, :], o2tm[mq][:, :], m3, r3,
                                        ALU.subtract, ALU.mult),
                                    deps=[e_rs3], inc=("VE", 1))
                        e_nrm_prev[mq] = e_n
                        deps_f = [e_n, e_x1, e_outdma.get(mglob - 2)]
                        e_fin[mglob] = P.add("vector",
                                             lambda v, par=par: v.tensor_add(
                                                 outst[par][:, :], t2b[:, :],
                                                 x1t[par][:, :]),
                                             deps=deps_f, inc=("VE", 1))
                        e_outdma[mglob] = P.add(
                            "sync",
                            lambda sy, mglob=mglob, par=par: sy.dma_start(
                                out[mglob * 128:(mglob + 1) * 128, :], outst[par][:, :]),
                            deps=[e_fin[mglob]], inc=("OUT", 16), dma=True)
                join_prev = P.join()

        out_total = sum(i[3][1] for i in P.events if i[3] and i[3][0] == "OUT")
        sc_total = sum(i[3][1] for i in P.events if i[3] and i[3][0] == "SC")

        def final_wait(sy):
            if out_total:
                sy.wait_ge(P.sems["OUT"], out_total)
            if sc_total:
                sy.wait_ge(P.sems["SC"], sc_total)
            return sy.nop()
        P.add("sync", final_wait, deps=(), inc=None)

        with nc.Block() as block:
            P.emit_all(block)

    return nc



def _host_layouts(Wq, Wk, Wv, Wo, w_gate, w_up, w_down):
    wqT = np.ascontiguousarray(Wq.T)
    wkT = np.ascontiguousarray(Wk.T)
    wvT = np.ascontiguousarray(Wv.T)
    woT = np.ascontiguousarray(Wo.T)

    def gu_tiles(w):
        r = w.reshape(D, 2048, 4)
        r = r.transpose(2, 0, 1)
        r = r.reshape(4, NDC, 128, 16, 128)
        r = r.transpose(0, 3, 1, 2, 4)
        return np.ascontiguousarray(r)
    wg_t = gu_tiles(w_gate)
    wu_t = gu_tiles(w_up)
    wdp = w_down.reshape(2048, 4, D).transpose(1, 0, 2).reshape(II, D)
    wd_t = np.ascontiguousarray(
        wdp.reshape(64, 128, NDC, 128).transpose(2, 0, 1, 3))
    return wqT, wkT, wvT, woT, wg_t, wu_t, wd_t


def _host_masks(p):
    G = GSETS[p]
    mk = np.zeros((4, 4, 128, 1024), dtype=np.float32)
    for j in range(4):
        g = G[j]
        for gi in range(4):
            for s2 in range(2):
                c = 8 * j + 2 * gi + s2
                keys = 128 * c + np.arange(128)[:, None]
                qs = 512 * g + np.arange(512)[None, :]
                mk[j, gi, :, s2 * 512:(s2 + 1) * 512] = (keys <= qs)
    return mk.astype(ml_dtypes.bfloat16)


def _make_in_maps(x, ws, mks):
    wqT, wkT, wvT, woT, wg_t, wu_t, wd_t = ws
    in_maps = []
    for c in range(8):
        b, p = divmod(c, 2)
        G = GSETS[p]
        x_own = np.concatenate([x[b, 512 * g:512 * (g + 1)] for g in G], axis=0)
        in_maps.append({
            "x_own": np.ascontiguousarray(x_own),
            "x_seq": np.ascontiguousarray(x[b]),
            "wqT": wqT, "wkT": wkT, "wvT": wvT, "woT": woT,
            "wg_t": wg_t, "wu_t": wu_t, "wd_t": wd_t,
            "masks": mks[p],
        })
    return in_maps


_NC_CACHE = {}


def _get_nc(key=("F2", ())):
    if key not in _NC_CACHE:
        _NC_CACHE[key] = build_nc(upto=key[0], dump_scr=key[1])
    return _NC_CACHE[key]


def kernel(L0_attn_x, L0_attn_Wq, L0_attn_Wk, L0_attn_Wv, L0_attn_Wo,
           L0_ffn_w_gate, L0_ffn_w_up, L0_ffn_w_down):
    x = np.asarray(L0_attn_x, dtype=np.float32)
    ws = _host_layouts(np.asarray(L0_attn_Wq, np.float32),
                       np.asarray(L0_attn_Wk, np.float32),
                       np.asarray(L0_attn_Wv, np.float32),
                       np.asarray(L0_attn_Wo, np.float32),
                       np.asarray(L0_ffn_w_gate, np.float32),
                       np.asarray(L0_ffn_w_up, np.float32),
                       np.asarray(L0_ffn_w_down, np.float32))
    mks = [_host_masks(0), _host_masks(1)]
    in_maps = _make_in_maps(x, ws, mks)
    nc = _get_nc()
    res = run_bass_kernel_spmd(nc, in_maps, core_ids=list(range(8)))
    y = np.empty((B, T, D), dtype=np.float32)
    for c in range(8):
        b, p = divmod(c, 2)
        G = GSETS[p]
        o = res.results[c]["out"]
        for j, g in enumerate(G):
            y[b, 512 * g:512 * (g + 1)] = o[512 * j:512 * (j + 1)]
    return y
